# version 14 (final)
# speedup vs baseline: 1.7369x; 1.0544x over previous
import numpy as np
import ml_dtypes
import concourse.bass as bass
import concourse.mybir as mybir
from concourse.bass_utils import run_bass_kernel_spmd
from contextlib import ExitStack


"""Mini auto-sync framework for raw-Bass megakernel construction.

Dep keys are arbitrary hashables, tracked at key granularity.

Streams:
  - compute stream per engine: ("eng", "c"); in-order completion, partial
    waits on the stream's semaphore are sound.
  - DMA channels: ("eng", "d", ch). DMA completions within a channel are
    NOT ordered, so a consumer always waits for the FULL cumulative count
    at its record time (16 per DMA), and the wait closes a "generation":
    the next DMA recorded on that channel first waits for the closed
    count itself, so later completions can never inflate the semaphore
    past a consumer's wait value before the consumer's target DMAs are
    all done. Use different channels for independent double-buffered
    streams to keep prefetch pipelining.

tensor->tensor (PE->PE) edges are always elided (PE is in-order,
accumulation-FIFO safe). Other same-engine compute->compute edges are kept
unless same_engine_sync=False.
"""

ENGINES = ("sync", "vector", "scalar", "tensor", "gpsimd")


class _Op:
    __slots__ = ("stream", "fn", "deps", "ordinal", "inc", "clock", "tag", "pre_waits")

    def __init__(self, stream, fn, ordinal, tag=None):
        self.stream = stream
        self.fn = fn
        self.deps = []
        self.ordinal = ordinal
        self.inc = False
        self.clock = None
        self.tag = tag
        self.pre_waits = []


class MK:
    def __init__(self, nc, same_engine_sync=True):
        self.nc = nc
        self.same_engine_sync = same_engine_sync
        self.ops = {}
        self.stream_of_engine = {e: [] for e in ENGINES}
        self.last_write = {}
        self.readers = {}
        self.clock = {e: {} for e in ENGINES}
        self.chan_closed = {}
        self.chan_boundary_applied = {}
        self.n_ops = 0

    def _oplist(self, stream):
        if stream not in self.ops:
            self.ops[stream] = []
        return self.ops[stream]

    def _record(self, eng, stream, fn, reads, writes, tag):
        oplist = self._oplist(stream)
        op = _Op(stream, fn, len(oplist), tag)
        myc = self.clock[eng]
        deps = {}

        def need(src_stream, n2):
            if src_stream == stream and (
                stream == ("tensor", "c") or
                (not self.same_engine_sync and stream[1] == "c")
            ):
                return
            if src_stream[1] == "d":
                n_full = len(self._oplist(src_stream)) - 1
                n2 = n_full
            if myc.get(src_stream, -1) >= n2:
                return
            if deps.get(src_stream, -1) < n2:
                deps[src_stream] = n2

        for k in reads:
            lw = self.last_write.get(k)
            if lw is not None:
                need(*lw)
        for k in writes:
            lw = self.last_write.get(k)
            if lw is not None:
                need(*lw)
            for (s2, n2) in self.readers.get(k, {}).items():
                need(s2, n2)

        op.deps = sorted(deps.items(), key=lambda x: str(x))
        for (s2, n2) in op.deps:
            src_clock = self.ops[s2][n2].clock
            for sk, sv in src_clock.items():
                if myc.get(sk, -1) < sv:
                    myc[sk] = sv
            if myc.get(s2, -1) < n2:
                myc[s2] = n2
            if s2[1] == "d":
                if self.chan_closed.get(s2, 0) < n2 + 1:
                    self.chan_closed[s2] = n2 + 1

        if stream[1] == "d":
            closed = self.chan_closed.get(stream, 0)
            if closed > self.chan_boundary_applied.get(stream, 0):
                op.pre_waits.append((stream, closed))
                self.chan_boundary_applied[stream] = closed

        me = (stream, op.ordinal)
        for k in writes:
            self.last_write[k] = me
            self.readers[k] = {}
        for k in reads:
            self.readers.setdefault(k, {})[stream] = op.ordinal
        snap = dict(myc)
        snap[stream] = op.ordinal
        op.clock = snap
        oplist.append(op)
        self.stream_of_engine[eng].append(op)
        self.n_ops += 1
        return op

    def op(self, eng, fn, reads=(), writes=(), tag=None):
        assert eng in ENGINES
        return self._record(eng, (eng, "c"), fn, reads, writes, tag)

    def barrier(self):
        targets = {s: len(lst) - 1 for s, lst in self.ops.items() if lst}
        for e in ENGINES:
            stream = (e, "c")
            oplist = self._oplist(stream)
            myc = self.clock[e]
            deps = []
            for s2, n2 in sorted(targets.items(), key=lambda x: str(x)):
                if s2 == stream:
                    continue
                if myc.get(s2, -1) >= n2:
                    continue
                deps.append((s2, n2))
                if s2[1] == "d" and self.chan_closed.get(s2, 0) < n2 + 1:
                    self.chan_closed[s2] = n2 + 1
            if not deps:
                continue
            op = _Op(stream, lambda eng: eng.nop(), len(oplist), "barrier")
            op.deps = deps
            for (s2, n2) in deps:
                src_clock = self.ops[s2][n2].clock
                for sk, sv in src_clock.items():
                    if myc.get(sk, -1) < sv:
                        myc[sk] = sv
                if myc.get(s2, -1) < n2:
                    myc[s2] = n2
            snap = dict(myc)
            snap[stream] = op.ordinal
            op.clock = snap
            oplist.append(op)
            self.stream_of_engine[e].append(op)
            self.n_ops += 1

    def dma(self, eng, fn, reads=(), writes=(), tag=None, channel=0):
        assert eng in ENGINES
        return self._record(eng, (eng, "d", channel), fn, reads, writes, tag)

    def finalize(self, final_wait_engine="sync"):
        nc = self.nc
        for s, lst in self.ops.items():
            for op in lst:
                for (s2, n2) in op.deps:
                    self.ops[s2][n2].inc = True
        for s, lst in self.ops.items():
            if s[1] == "d":
                for op in lst:
                    op.inc = True
        sems = {}
        for s, lst in self.ops.items():
            if lst:
                nm = "mk_" + "_".join(str(x) for x in s)
                sems[s] = nc.alloc_semaphore(nm)
        inc_cum = {}
        for s, lst in self.ops.items():
            c = 0
            cum = []
            for op in lst:
                if op.inc:
                    c += 1
                cum.append(c)
            inc_cum[s] = cum

        def unit(s):
            return 16 if s[1] == "d" else 1

        self.final_counts = {
            s: (inc_cum[s][-1] if inc_cum[s] else 0) * unit(s) for s in self.ops
        }
        self.sems = sems

        with nc.Block() as block:
            def mk_section(e):
                def body(engobj):
                    for op in self.stream_of_engine[e]:
                        waits = [(sems[s2], n * 16) for (s2, n) in op.pre_waits]
                        waits += [(sems[s2], inc_cum[s2][n2] * unit(s2))
                                  for (s2, n2) in op.deps]
                        for (sem, val) in waits[:-1]:
                            engobj.wait_ge(sem, val)
                        if isinstance(op.fn, tuple):
                            method, kwargs = op.fn
                            ins = getattr(engobj, method)(**kwargs)
                        else:
                            ins = op.fn(engobj)
                        if waits:
                            ins._wait_ge(waits[-1][0], waits[-1][1])
                        if op.inc:
                            ins.then_inc(sems[op.stream], unit(op.stream))
                    if e == final_wait_engine:
                        for s2, sem in sems.items():
                            if s2[1] == "d" and self.final_counts.get(s2, 0) > 0:
                                engobj.wait_ge(sem, self.final_counts[s2])
                return body

            for e in ENGINES:
                if not self.stream_of_engine[e] and e != final_wait_engine:
                    continue
                getattr(block, e if e != "sync" else "sync")(mk_section(e))
        return self

"""Common helpers for the megakernel blocks.

Layout conventions (per core, one sequence of T tokens, D=1024):
 - stream x_sb: SBUF fp32 [128, T/128 * 1024]; chunk j (tokens 128j..)
   occupies columns [j*1024, (j+1)*1024)  (token-major: partition=token,
   free=feature).
 - t-major tile: [128 tokens, D]
 - f-major activations: per c-chunk [128 features, ntok]
 - All matmul operands start at partition base 0; accumulation groups are
   per single PSUM bank; one accumulation group per bank at a time.
"""

F32 = mybir.dt.float32
BF16 = mybir.dt.bfloat16
AF = mybir.ActivationFunctionType
ALU = mybir.AluOpType
AX = mybir.AxisListType

D = 1024
NC = D // 128
LN_EPS = 1e-5


class Ctx:

    def __init__(self, nc, mk, T):
        self.nc = nc
        self.mk = mk
        self.T = T
        self.NT = T // 128
        self.NTILE = T // 512
        self.uid = 0

    def name(self, base):
        self.uid += 1
        return f"{base}_{self.uid}"

    def OP(self, eng, method, reads=(), writes=(), **kwargs):
        return self.mk.op(eng, (method, kwargs), reads=reads, writes=writes)

    def DMA(self, eng, out, in_, reads=(), writes=(), channel=0, **kw):
        return self.mk.dma(eng, ("dma_start", dict(out=out, in_=in_, **kw)),
                           reads=reads, writes=writes, channel=channel)

    def sbuf(self, name, shape, dtype=F32):
        scope = getattr(self, "block_scope", None)
        if scope is not None:
            return scope.enter_context(self.nc.sbuf_tensor(name, shape, dtype)).ap()
        return self.nc.alloc_sbuf_tensor(name, shape, dtype).ap()

    def constf(self, val, dtype=F32):
        key = (dtype, float(val))
        if not hasattr(self, "_consts"):
            self._consts = {}
        if key in self._consts:
            return
        ap = self.nc.alloc_sbuf_tensor(self.name("c_f"), [128, 1], dtype).ap()
        self.mk.op("vector", lambda e: e.memset(ap, float(val)), writes=["c_pool"])
        self.nc.const_aps.register(dtype, float(val), ap)
        self._consts[key] = ap


def setup_psum(ctx):
    nc = ctx.nc
    ctx.pb = [nc.alloc_psum_tensor(f"pb{i}", [128, 512], F32).ap() for i in range(6)]
    ctx.pt = [nc.alloc_psum_tensor(f"pt{i}", [128, 1024], BF16).ap() for i in range(2)]
    ctx.pb_key = [f"ps{i}" for i in range(6)]
    ctx.pt_key = ["pst0", "pst1"]


def setup_consts(ctx, ident_f32, ident_bf16, iota_row=None):
    mk = ctx.mk
    ctx.ident_f32 = ctx.sbuf("c_idf", [128, 128], F32)
    ctx.ident_bf16 = ctx.sbuf("c_idb", [128, 128], BF16)
    ctx.ones_col = ctx.sbuf("c_ones", [128, 1], F32)
    ctx.ones_col_bf = ctx.sbuf("c_onesb", [128, 1], BF16)
    ctx.ones_row_bf = ctx.sbuf("c_onesrow", [1, 512], BF16)
    ctx.ones_row_f32 = ctx.sbuf("c_1rowf", [1, 128], F32)
    ctx.ones2_row_bf = ctx.sbuf("c_ones2", [2, 128], BF16)
    mk.dma("sync", lambda e: e.dma_start(out=ctx.ident_f32, in_=ident_f32), writes=["c_idf"])
    mk.dma("sync", lambda e: e.dma_start(out=ctx.ident_bf16, in_=ident_bf16), writes=["c_idb"])
    mk.op("vector", lambda e: e.memset(ctx.ones_col, 1.0), writes=["c_ones"])
    mk.op("vector", lambda e: e.memset(ctx.ones_col_bf, 1.0), writes=["c_onesb"])
    mk.op("vector", lambda e: e.memset(ctx.ones_row_bf, 1.0), writes=["c_onesrow"])
    mk.op("vector", lambda e: e.memset(ctx.ones_row_f32, 1.0), writes=["c_1rowf"])
    mk.op("vector", lambda e: e.memset(ctx.ones2_row_bf, 1.0), writes=["c_ones2"])
    for v in (1e-5, 1e-6, -30.0, float(np.pi) / 2):
        ctx.constf(v)


def transpose128(ctx, out_ap, out_key, in_ap, in_key, dtype, pt_idx=0, pt_slot=0,
                 evac_engine="vector"):
    mk = ctx.mk
    if dtype == BF16:
        stage = ctx.pt[pt_idx][:, pt_slot * 128:(pt_slot + 1) * 128]
        skey = ctx.pt_key[pt_idx]
        ident = ctx.ident_bf16
        ikey = "c_idb"
    else:
        stage = ctx.pb[5][:, pt_slot * 128:(pt_slot + 1) * 128]
        skey = ctx.pb_key[5]
        ident = ctx.ident_f32
        ikey = "c_idf"
    mk.op("tensor", lambda e: e.transpose(out=stage, in_=in_ap, identity=ident),
          reads=[in_key, ikey], writes=[skey])
    mk.op(evac_engine, lambda e: e.tensor_copy(out=out_ap, in_=stage),
          reads=[skey], writes=[out_key])


def emit_rsqrt(ctx, out_ap, in_ap, key, eps, engine="scalar"):
    mk = ctx.mk
    ctx.constf(eps)
    mk.op(engine, lambda e: e.activation(out=out_ap, in_=in_ap, func=AF.Ln, bias=float(eps)),
          reads=[key, "c_pool"], writes=[key])
    mk.op(engine, lambda e: e.activation(out=out_ap, in_=out_ap, func=AF.Exp, scale=-0.5),
          reads=[key], writes=[key])


def emit_ln_stats_tm(ctx, x_ap, x_key, mu_ap, rstd_ap, stat_key, scratch_ap, scratch_key,
                     n_feat, eps=LN_EPS):
    mk = ctx.mk
    inv = 1.0 / n_feat
    xks = x_key if isinstance(x_key, list) else [x_key]
    sks = scratch_key if isinstance(scratch_key, list) else [scratch_key]
    mk.op("vector", lambda e: e.reduce_sum(out=mu_ap, in_=x_ap, axis=AX.X),
          reads=xks, writes=[stat_key])
    mk.op("scalar", lambda e: e.activation(out=scratch_ap, in_=x_ap, func=AF.Square),
          reads=xks, writes=sks)
    mk.op("vector", lambda e: e.reduce_sum(out=rstd_ap, in_=scratch_ap, axis=AX.X),
          reads=sks, writes=[stat_key])
    mk.op("vector", lambda e: e.tensor_scalar(out=mu_ap, in0=mu_ap, scalar1=inv,
                                              scalar2=None, op0=ALU.mult),
          reads=[stat_key], writes=[stat_key])
    mk.op("vector", lambda e: e.scalar_tensor_tensor(out=scratch_ap[:, 0:1], in0=mu_ap,
                                                     scalar=-1.0, in1=mu_ap,
                                                     op0=ALU.mult, op1=ALU.mult),
          reads=[stat_key], writes=sks)
    mk.op("vector", lambda e: e.scalar_tensor_tensor(out=rstd_ap, in0=rstd_ap,
                                                     scalar=inv, in1=scratch_ap[:, 0:1],
                                                     op0=ALU.mult, op1=ALU.add),
          reads=[stat_key] + sks, writes=[stat_key])

"""Host-side layout-only input preparation (transposes/reshapes/consts)."""

BF = ml_dtypes.bfloat16
H16 = 16


def host_prep(w, T):
    d = {}
    d["ident"] = np.eye(128, dtype=np.float32)
    d["ident_bf"] = np.eye(128, dtype=np.float32).astype(BF)
    d["qk_wT"] = np.ascontiguousarray(w["L0_attn_qk_w"].T)
    d["v_wT"] = np.ascontiguousarray(w["L0_attn_v_w"].T)
    d["g_wT"] = np.ascontiguousarray(w["L0_attn_g_w"].T)
    d["o_wT"] = np.ascontiguousarray(w["L0_attn_o_w"].T)
    cw = np.asarray(w["L0_attn_conv_w"]).reshape(1024, 4)
    d["conv_w_col"] = np.ascontiguousarray(
        cw.reshape(8, 128, 4).transpose(1, 0, 2).reshape(128, 32))
    d["conv_b_col"] = np.ascontiguousarray(np.asarray(w["L0_attn_conv_b"]).reshape(8, 128).T)
    gb = np.asarray(w["L0_attn_g_b"], dtype=np.float32)
    hi = gb.astype(BF).astype(np.float32)
    lo = (gb - hi).astype(BF).astype(np.float32)
    d["g_b_row2"] = np.stack([hi, lo]).astype(BF)
    d["f_row"] = np.ascontiguousarray(np.asarray(w["L0_attn_freqs"]).reshape(1, 2048))
    d["gamma_row"] = np.ascontiguousarray(np.asarray(w["L0_attn_gamma"]).reshape(1, 16))
    d["ln_g_row"] = np.ascontiguousarray(np.asarray(w["L0_attn_ln_g"]).reshape(1, 1024))
    d["ln_b_row"] = np.ascontiguousarray(np.asarray(w["L0_attn_ln_b"]).reshape(1, 1024))
    d["iota_col"] = np.ascontiguousarray(
        np.arange(T, dtype=np.float32).reshape(T // 128, 128).T)
    p = np.arange(128)
    d["ltri"] = (p[:, None] <= p[None, :]).astype(np.float32)
    d["w1T"] = np.ascontiguousarray(w["L0_ffn_w1"].T)
    d["w2T"] = np.ascontiguousarray(w["L0_ffn_w2"].T)
    d["ln1_g_col"] = np.ascontiguousarray(np.asarray(w["L0_ffn_ln1_g"]).reshape(8, 128).T)
    d["ln1_b_col"] = np.ascontiguousarray(np.asarray(w["L0_ffn_ln1_b"]).reshape(8, 128).T)
    d["ln2_g_col"] = np.ascontiguousarray(np.asarray(w["L0_ffn_ln2_g"]).reshape(32, 128).T)
    d["ln2_b_col"] = np.ascontiguousarray(np.asarray(w["L0_ffn_ln2_b"]).reshape(32, 128).T)
    d["b1_col"] = np.ascontiguousarray(np.asarray(w["L0_ffn_b1"]).reshape(32, 128).T)
    d["b2_row"] = np.ascontiguousarray(np.asarray(w["L0_ffn_b2"]).reshape(1, 1024))
    d["WqT"] = np.ascontiguousarray(w["L1_attn_Wq"].T)
    d["WkT"] = np.ascontiguousarray(w["L1_attn_Wk"].T)
    d["WvT"] = np.ascontiguousarray(w["L1_attn_Wv"].T)
    d["WoT"] = np.ascontiguousarray(w["L1_attn_Wo"].T)
    d["bo_col"] = np.ascontiguousarray(np.asarray(w["L1_attn_bo"]).reshape(8, 128).T)
    m = np.zeros((128, 4 * 512), np.float32)
    q = np.arange(512)[None, :]
    for dd in range(4):
        m[:, dd * 512:(dd + 1) * 512] = (p[:, None] + dd * 128 <= q)
    d["l1a_masks"] = m.astype(BF)
    sl = 2.0 ** (-np.arange(1, H16 + 1) * 8.0 / H16)
    tk = np.arange(T, dtype=np.float64)
    ak = np.zeros((32, T), np.float32)
    aq = np.zeros((32, T), np.float32)
    for h in range(H16):
        v = (sl[h] * tk).astype(np.float32)
        vhi = v.astype(BF).astype(np.float32)
        ak[2 * h], ak[2 * h + 1] = vhi, (v - vhi).astype(BF).astype(np.float32)
        vq = (-sl[h] * tk).astype(np.float32)
        vqh = vq.astype(BF).astype(np.float32)
        aq[2 * h], aq[2 * h + 1] = vqh, (vq - vqh).astype(BF).astype(np.float32)
    d["ak_rows"] = ak.astype(BF)
    d["aq_rows"] = aq.astype(BF)
    d["moe_w1"] = np.ascontiguousarray(w["L1_ffn_w1"])
    d["moe_w2"] = np.ascontiguousarray(w["L1_ffn_w2"])
    d["moe_b1_col"] = np.ascontiguousarray(
        np.asarray(w["L1_ffn_b1"]).reshape(8, 32, 128).transpose(2, 0, 1).reshape(128, 256))
    d["moe_b2_col"] = np.ascontiguousarray(
        np.asarray(w["L1_ffn_b2"]).reshape(8, 8, 128).transpose(2, 0, 1).reshape(128, 64))
    d["moe_gamma"] = np.asarray(w["L1_ffn_gamma"], dtype=np.float32).astype(BF)
    d["moe_beta"] = np.asarray(w["L1_ffn_beta"], dtype=np.float32).astype(BF)
    d["moe_gamma_col"] = np.ascontiguousarray(
        np.asarray(w["L1_ffn_gamma"]).reshape(8, 8, 128).transpose(2, 0, 1).reshape(128, 64))
    d["moe_router_col"] = np.ascontiguousarray(
        np.asarray(w["L1_ffn_router_w"]).reshape(8, 128, 8).transpose(1, 0, 2).reshape(128, 64))
    return d


F32S = "float32"
BF16S = "bfloat16"
SPECS = {
    "common": [("ident", lambda T: [128, 128], F32S), ("ident_bf", lambda T: [128, 128], BF16S)],
    "l0a": [("qk_wT", lambda T: [1024, 2048], F32S), ("v_wT", lambda T: [1024, 2048], F32S),
            ("g_wT", lambda T: [1024, 2048], F32S), ("o_wT", lambda T: [2048, 1024], F32S),
            ("conv_w_col", lambda T: [128, 32], F32S), ("conv_b_col", lambda T: [128, 8], F32S),
            ("g_b_row2", lambda T: [2, 2048], BF16S), ("f_row", lambda T: [1, 2048], F32S),
            ("gamma_row", lambda T: [1, 16], F32S), ("ln_g_row", lambda T: [1, 1024], F32S),
            ("ln_b_row", lambda T: [1, 1024], F32S), ("iota_col", lambda T: [128, T // 128], F32S),
            ("ltri", lambda T: [128, 128], F32S)],
    "l0f": [("w1T", lambda T: [1024, 4096], F32S), ("w2T", lambda T: [4096, 1024], F32S),
            ("ln1_g_col", lambda T: [128, 8], F32S), ("ln1_b_col", lambda T: [128, 8], F32S),
            ("ln2_g_col", lambda T: [128, 32], F32S), ("ln2_b_col", lambda T: [128, 32], F32S),
            ("b1_col", lambda T: [128, 32], F32S), ("b2_row", lambda T: [1, 1024], F32S)],
    "l1a": [("WqT", lambda T: [1024, 1024], F32S), ("WkT", lambda T: [1024, 1024], F32S),
            ("WvT", lambda T: [1024, 1024], F32S), ("WoT", lambda T: [1024, 1024], F32S),
            ("bo_col", lambda T: [128, 8], F32S), ("l1a_masks", lambda T: [128, 2048], BF16S),
            ("ak_rows", lambda T: [32, T], BF16S), ("aq_rows", lambda T: [32, T], BF16S)],
    "moe": [("moe_w1", lambda T: [8, 1024, 4096], F32S), ("moe_w2", lambda T: [8, 4096, 1024], F32S),
            ("moe_b1_col", lambda T: [128, 256], F32S), ("moe_b2_col", lambda T: [128, 64], F32S),
            ("moe_gamma", lambda T: [8, 1024], BF16S), ("moe_beta", lambda T: [8, 1024], BF16S),
            ("moe_gamma_col", lambda T: [128, 64], F32S), ("moe_router_col", lambda T: [128, 64], F32S)],
}

"""L0_attn (retnet-style) block: x += LN0(mix(LN0(x))).

mix(xin):
  xc = causal_depthwise_conv4(xin) + conv_b
  qk = xc @ qk_w.T ; q = qk / max(||qk||_head, 1e-12) ; k = q
  v = xc @ v_w.T ; g = sigmoid(xc @ g_w.T + g_b)
  k_sh = shift1(k); state = v*(cos(t*f) + k_sh)*g
  u = cumsum_t(state * gamma^t); mem = u / (gamma^t + 1e-6)
  a = mem * q ; o = a @ o_w.T ; out = LN(o)*ln_g + ln_b

Layout: recurrence core token-major [128 t, 2048 e]; conv f-major;
projections t-major out via lhsT = xc_fm chunks; weights streamed per
2-chunk tile from bf16 blobs (natural [in, out] layout).
cos: a = t*f (fp32, matches ref rounding), m = floor((a+pi/2)/2pi + 0.5)
via python_mod (fp64-safe), r = a - m*C1 - m*C2, cos = Sin(r + pi/2).
"""

PI = float(np.pi)
PI_LO = float(np.nextafter(np.float32(np.pi), np.float32(0)))
C1 = 6.28125
C2 = float(np.float32(2 * np.pi - 6.28125))
INV2PI = float(1.0 / (2 * np.pi))


def emit_l0a_prologue(ctx, din):
    nc, mk = ctx.nc, ctx.mk
    OP, DMA = ctx.OP, ctx.DMA
    T = ctx.T
    s = {}
    ctx.l0a = s
    s["qkw"] = nc.dram_tensor("l0a_qkw", [1024, 2048], BF16, kind="Internal").ap()
    s["vw"] = nc.dram_tensor("l0a_vw", [1024, 2048], BF16, kind="Internal").ap()
    s["gw"] = nc.dram_tensor("l0a_gw", [1024, 2048], BF16, kind="Internal").ap()
    s["ow"] = nc.dram_tensor("l0a_ow", [2048, 1024], BF16, kind="Internal").ap()

    s["convw"] = ctx.sbuf("l0a_convw", [128, 32], F32)
    s["convb"] = ctx.sbuf("l0a_convb", [128, 8], F32)
    s["gb"] = ctx.sbuf("l0a_gb", [2, 2048], BF16)
    s["fbc"] = ctx.sbuf("l0a_fbc", [128, 2048], F32)
    s["tcol"] = ctx.sbuf("l0a_tcol", [128, ctx.NT], F32)
    s["dk"] = ctx.sbuf("l0a_dk", [128, 16 * ctx.NT], F32)
    s["rd"] = ctx.sbuf("l0a_rd", [128, 16 * ctx.NT], F32)
    s["lng_bc"] = ctx.sbuf("l0a_lngbc", [128, 1024], F32)
    s["lnb_bc"] = ctx.sbuf("l0a_lnbbc", [128, 1024], F32)
    s["ltri"] = ctx.sbuf("l0a_ltri", [128, 128], F32)

    DMA("sync", s["convw"], din["conv_w_col"], writes=["l0a_convw"])
    DMA("sync", s["convb"], din["conv_b_col"], writes=["l0a_convb"])
    DMA("sync", s["gb"], din["g_b_row2"], writes=["l0a_gb"])
    DMA("sync", s["tcol"], din["iota_col"], writes=["l0a_tcol"])
    DMA("sync", s["ltri"], din["ltri"], writes=["l0a_ltri"])

    _scope = ExitStack()
    def scoped(name, shape, dt):
        return _scope.enter_context(nc.sbuf_tensor(name, shape, dt)).ap()

    win = scoped("l0a_win", [128, 2048], F32)
    wbf = scoped("l0a_wbf", [128, 2048], BF16)
    grow = scoped("l0a_grow", [1, 1024], F32)
    DMA("sync", grow, din["ln_g_row"], writes=["l0a_grow"])
    brow = scoped("l0a_brow", [1, 1024], F32)
    DMA("sync", brow, din["ln_b_row"], writes=["l0a_brow"])
    for gseg in range(2):
        OP("tensor", "matmul", reads=["l0a_grow", "c_1rowf"], writes=["ps0"],
           out=ctx.pb[0][:, 0:512], lhsT=ctx.ones_row_f32,
           rhs=grow[:, gseg * 512:(gseg + 1) * 512], start=True, stop=True)
        OP("vector", "tensor_copy", reads=["ps0"], writes=["l0a_lngbc"],
           out=s["lng_bc"][:, gseg * 512:(gseg + 1) * 512], in_=ctx.pb[0][:, 0:512])
        OP("tensor", "matmul", reads=["l0a_brow", "c_1rowf"], writes=["ps1"],
           out=ctx.pb[1][:, 0:512], lhsT=ctx.ones_row_f32,
           rhs=brow[:, gseg * 512:(gseg + 1) * 512], start=True, stop=True)
        OP("vector", "tensor_copy", reads=["ps1"], writes=["l0a_lnbbc"],
           out=s["lnb_bc"][:, gseg * 512:(gseg + 1) * 512], in_=ctx.pb[1][:, 0:512])
    for nm, src_name, nch in [("qkw", "qk_wT", 8), ("vw", "v_wT", 8), ("gw", "g_wT", 8)]:
        for c in range(nch):
            DMA("sync", win, din[src_name][c * 128:(c + 1) * 128, :],
                reads=["l0a_win"], writes=["l0a_win"], channel=1)
            OP("vector", "tensor_copy", reads=["l0a_win"], writes=["l0a_wbf"],
               out=wbf, in_=win)
            DMA("sync", s[nm][c * 128:(c + 1) * 128, :], wbf,
                reads=["l0a_wbf"], writes=[f"l0a_{nm}b"], channel=2)
    for c in range(16):
        DMA("sync", win[:, 0:1024], din["o_wT"][c * 128:(c + 1) * 128, :],
            reads=["l0a_win"], writes=["l0a_win"], channel=1)
        OP("vector", "tensor_copy", reads=["l0a_win"], writes=["l0a_wbf"],
           out=wbf[:, 0:1024], in_=win[:, 0:1024])
        DMA("sync", s["ow"][c * 128:(c + 1) * 128, :], wbf[:, 0:1024],
            reads=["l0a_wbf"], writes=["l0a_owb"], channel=2)

    frow = scoped("l0a_frow", [1, 2048], F32)
    DMA("sync", frow, din["f_row"], writes=["l0a_frow"])
    for gseg in range(4):
        OP("tensor", "matmul", reads=["l0a_frow", "c_1rowf"], writes=["ps0"],
           out=ctx.pb[0][:, 0:512], lhsT=ctx.ones_row_f32,
           rhs=frow[:, gseg * 512:(gseg + 1) * 512], start=True, stop=True)
        OP("vector", "tensor_copy", reads=["ps0"], writes=["l0a_fbc"],
           out=s["fbc"][:, gseg * 512:(gseg + 1) * 512], in_=ctx.pb[0][:, 0:512])
    garow = scoped("l0a_garow", [1, 16], F32)
    DMA("sync", garow, din["gamma_row"], writes=["l0a_garow"])
    OP("scalar", "activation", reads=["l0a_garow"], writes=["l0a_garow"],
       out=garow, in_=garow, func=AF.Ln)
    irow = scoped("l0a_irow", [1, 128], F32)
    for j in range(ctx.NT):
        OP("tensor", "transpose", reads=["l0a_tcol", "c_idf"], writes=["ps2"],
           out=ctx.pb[2][0:1, 0:128], in_=s["tcol"][:, j:j + 1], identity=ctx.ident_f32)
        OP("vector", "tensor_copy", reads=["ps2"], writes=["l0a_irow"],
           out=irow, in_=ctx.pb[2][0:1, 0:128])
        OP("tensor", "matmul", reads=["l0a_irow", "l0a_garow"], writes=["ps3"],
           out=ctx.pb[3][:, 0:16], lhsT=irow, rhs=garow, start=True, stop=True)
        OP("scalar", "activation", reads=["ps3"], writes=["l0a_dk"],
           out=s["dk"][:, j * 16:(j + 1) * 16], in_=ctx.pb[3][:, 0:16], func=AF.Exp)
        OP("vector", "tensor_scalar", reads=["l0a_dk"], writes=["l0a_rd"],
           out=s["rd"][:, j * 16:(j + 1) * 16], in0=s["dk"][:, j * 16:(j + 1) * 16],
           scalar1=1e-6, scalar2=None, op0=ALU.add)
        OP("vector", "reciprocal", reads=["l0a_rd"], writes=["l0a_rd"],
           out=s["rd"][:, j * 16:(j + 1) * 16], in_=s["rd"][:, j * 16:(j + 1) * 16])
    mk.barrier()
    _scope.close()


def emit_l0a(ctx, x_sb, xkey):
    nc, mk = ctx.nc, ctx.mk
    OP, DMA = ctx.OP, ctx.DMA
    s = ctx.l0a
    T, NT = ctx.T, ctx.NT
    _sc = ExitStack()
    def sc(name, shape, dt):
        return _sc.enter_context(nc.sbuf_tensor(name, shape, dt)).ap()

    ctx.constf(PI / 2)

    xin_fm = sc("l0a_xinfm", [128, 8 * 131], F32)
    xc_bf = sc("l0a_xcbf0", [128, 1024], BF16)
    wtile = [sc(f"l0a_wtile{i}", [128, 512], BF16) for i in range(2)]
    qk_bf = sc("l0a_qk0", [128, 2048], BF16)
    v_bf = sc("l0a_v0", [128, 2048], BF16)
    g_bf = sc("l0a_g0", [128, 2048], BF16)
    ksh = sc("l0a_ksh", [128, 2048], BF16)
    crow = sc("l0a_crow", [1, 2048], F32)
    a_fm = sc("l0a_afm0", [128, 2048], BF16)
    esA = sc("l0a_esA", [128, 4 * 512], F32)
    esM = sc("l0a_esM", [128, 4 * 512], F32)
    esT = None
    ssq = sc("l0a_ssq", [128, 16], F32)
    stats = sc("l0a_stats", [128, 8], F32)
    esI = xc_bf.bitcast(mybir.dt.int32)
    esF = a_fm.bitcast(F32)[:, 0:512]
    escr = esM[:, 0:1024]
    ESK = [("l0a_esM", 0), ("l0a_esM", 1)]
    of = esA[:, 0:1024]
    OFK = [("l0a_esA", 0), ("l0a_esA", 1)]

    OP("vector", "memset", writes=["l0a_crow"], ap=crow, constant=0.0)
    OP("vector", "memset", writes=["l0a_ksh"], ap=ksh[0:1, :], constant=0.0)
    OP("vector", "memset", writes=["l0a_xinfm"], ap=xin_fm, constant=0.0)

    wpar = 0
    for j in range(NT):
        xck = (xkey, j)
        xc_ = x_sb[:, j * 1024:(j + 1) * 1024]
        mu, var = stats[:, 0:1], stats[:, 1:2]
        emit_ln_stats_tm(ctx, xc_, xck, mu, var, "l0a_stats", escr, ESK, 1024)
        emit_rsqrt(ctx, var, var, "l0a_stats", LN_EPS)
        OP("vector", "scalar_tensor_tensor", reads=["l0a_stats"], writes=["l0a_stats"],
           out=stats[:, 2:3], in0=mu, scalar=-1.0, in1=var, op0=ALU.mult, op1=ALU.mult)
        OP("vector", "tensor_scalar", reads=[xck, "l0a_stats"], writes=OFK,
           out=of, in0=xc_, scalar1=var, scalar2=stats[:, 2:3],
           op0=ALU.mult, op1=ALU.add)
        for c in range(8):
            transpose128(ctx, xin_fm[:, c * 131 + 3: c * 131 + 131], "l0a_xinfm",
                         of[:, c * 128:(c + 1) * 128], OFK[c // 4],
                         F32, pt_slot=c % 4)
        for c in range(8):
            w = s["convw"]
            ek = ("l0a_esM", c % 4)
            esl = esM[:, (c % 4) * 512:(c % 4) * 512 + 128]
            OP("vector", "tensor_scalar", reads=["l0a_xinfm", "l0a_convw", "l0a_convb"],
               writes=[ek],
               out=esl, in0=xin_fm[:, c * 131 + 0: c * 131 + 128],
               scalar1=w[:, c * 4:c * 4 + 1], scalar2=s["convb"][:, c:c + 1],
               op0=ALU.mult, op1=ALU.add)
            for tap in (1, 2):
                OP("vector", "scalar_tensor_tensor",
                   reads=["l0a_xinfm", "l0a_convw", ek], writes=[ek],
                   out=esl, in0=xin_fm[:, c * 131 + tap: c * 131 + tap + 128],
                   scalar=w[:, c * 4 + tap:c * 4 + tap + 1], in1=esl,
                   op0=ALU.mult, op1=ALU.add)
            OP("vector", "scalar_tensor_tensor",
               reads=["l0a_xinfm", "l0a_convw", ek], writes=["l0a_xc"],
               out=xc_bf[:, c * 128:(c + 1) * 128],
               in0=xin_fm[:, c * 131 + 3: c * 131 + 131],
               scalar=w[:, c * 4 + 3:c * 4 + 4], in1=esl,
               op0=ALU.mult, op1=ALU.add)
        for c in range(8):
            OP("vector", "tensor_copy", reads=["l0a_xinfm"], writes=["l0a_xinfm"],
               out=xin_fm[:, c * 131: c * 131 + 3],
               in_=xin_fm[:, c * 131 + 128: c * 131 + 131])
        for nm, blob, outb, okey in [("qkw", s["qkw"], qk_bf, "l0a_qkb"),
                                     ("vw", s["vw"], v_bf, "l0a_vb"),
                                     ("gw", s["gw"], g_bf, "l0a_gb2")]:
            for eb in range(4):
                for c in range(8):
                    wt = wtile[wpar % 2]
                    wk = ("l0a_wtile", wpar % 2)
                    wpar += 1
                    DMA("sync", wt, blob[c * 128:(c + 1) * 128, eb * 512:(eb + 1) * 512],
                        reads=[wk, f"l0a_{nm}b"], writes=[wk], channel=1)
                    OP("tensor", "matmul",
                       reads=[wk, "l0a_xc"], writes=[f"ps{eb % 2}"],
                       out=ctx.pb[eb % 2][:, 0:512],
                       lhsT=xc_bf[:, c * 128:(c + 1) * 128],
                       rhs=wt, start=(c == 0),
                       stop=(c == 7 and nm != "gw"))
                if nm == "gw":
                    OP("tensor", "matmul", reads=["c_ones2", "l0a_gb"], writes=[f"ps{eb % 2}"],
                       out=ctx.pb[eb % 2][:, 0:512], lhsT=ctx.ones2_row_bf,
                       rhs=s["gb"][:, eb * 512:(eb + 1) * 512],
                       start=False, stop=True)
                    OP("scalar", "activation", reads=[f"ps{eb % 2}"], writes=[okey],
                       out=outb[:, eb * 512:(eb + 1) * 512], in_=ctx.pb[eb % 2][:, 0:512],
                       func=AF.Sigmoid)
                else:
                    OP("vector", "tensor_copy", reads=[f"ps{eb % 2}"], writes=[okey],
                       out=outb[:, eb * 512:(eb + 1) * 512], in_=ctx.pb[eb % 2][:, 0:512])
        for eq in range(4):
            OP("scalar", "activation", reads=["l0a_qkb"], writes=[("l0a_esM", eq)],
               out=esM[:, eq * 512:(eq + 1) * 512],
               in_=qk_bf[:, eq * 512:(eq + 1) * 512], func=AF.Square)
        for h in range(16):
            OP("vector", "reduce_sum", reads=[("l0a_esM", h // 4)], writes=["l0a_ssq"],
               out=ssq[:, h:h + 1], in_=esM[:, h * 128:(h + 1) * 128], axis=AX.X)
        OP("vector", "tensor_scalar", reads=["l0a_ssq"], writes=["l0a_ssq"],
           out=ssq, in0=ssq, scalar1=1e-24, scalar2=None, op0=ALU.max)
        OP("scalar", "activation", reads=["l0a_ssq"], writes=["l0a_ssq"],
           out=ssq, in_=ssq, func=AF.Ln)
        OP("scalar", "activation", reads=["l0a_ssq"], writes=["l0a_ssq"],
           out=ssq, in_=ssq, func=AF.Exp, scale=-0.5)
        OP("vector", "tensor_scalar", reads=["l0a_ssq"], writes=["l0a_ssq"],
           out=ssq, in0=ssq, scalar1=1e12, scalar2=None, op0=ALU.min)
        for h in range(16):
            OP("vector", "tensor_scalar", reads=["l0a_qkb", "l0a_ssq"],
               writes=["l0a_qkb"],
               out=qk_bf[:, h * 128:(h + 1) * 128], in0=qk_bf[:, h * 128:(h + 1) * 128],
               scalar1=ssq[:, h:h + 1], scalar2=None, op0=ALU.mult)
        DMA("sync", ksh[1:128, :], qk_bf[0:127, :], reads=["l0a_qkb", "l0a_ksh"],
            writes=["l0a_ksh"], channel=2)
        for eq in range(4):
            sl = slice(eq * 512, (eq + 1) * 512)
            kA, kM = ("l0a_esA", eq), ("l0a_esM", eq)
            eA = esA[:, sl]
            eM = esM[:, sl]
            OP("vector", "tensor_scalar", reads=["l0a_fbc", "l0a_tcol"], writes=[kA],
               out=eA, in0=s["fbc"][:, sl], scalar1=s["tcol"][:, j:j + 1],
               scalar2=None, op0=ALU.mult)
            OP("vector", "tensor_scalar", reads=[kA], writes=[kM],
               out=eM, in0=eA, scalar1=INV2PI, scalar2=(PI / 2) * INV2PI + 0.5,
               op0=ALU.mult, op1=ALU.add)
            OP("vector", "tensor_copy", reads=[kM], writes=["l0a_xc"], out=esI, in_=eM)
            OP("vector", "tensor_copy", reads=["l0a_xc"], writes=["l0a_afm"],
               out=esF, in_=esI)
            OP("vector", "tensor_sub", reads=[kM, "l0a_afm"], writes=[kM],
               out=eM, in0=eM, in1=esF)
            OP("vector", "tensor_scalar", reads=[kM], writes=[kM],
               out=eM, in0=eM, scalar1=0.0, scalar2=None, op0=ALU.is_lt)
            OP("vector", "tensor_sub", reads=["l0a_afm", kM], writes=[kM],
               out=eM, in0=esF, in1=eM)
            OP("vector", "scalar_tensor_tensor", reads=[kM, kA], writes=[kA],
               out=eA, in0=eM, scalar=-C1, in1=eA, op0=ALU.mult, op1=ALU.add)
            OP("vector", "scalar_tensor_tensor", reads=[kM, kA], writes=[kA],
               out=eA, in0=eM, scalar=-C2, in1=eA, op0=ALU.mult, op1=ALU.add)
            OP("vector", "tensor_scalar", reads=[kA], writes=[kA],
               out=eA, in0=eA, scalar1=PI / 2, scalar2=PI_LO, op0=ALU.add, op1=ALU.min)
            OP("vector", "tensor_scalar", reads=[kA], writes=[kA],
               out=eA, in0=eA, scalar1=-PI_LO, scalar2=None, op0=ALU.max)
            OP("scalar", "activation", reads=[kA], writes=[kA],
               out=eA, in_=eA, func=AF.Sin)
            OP("vector", "tensor_add", reads=[kA, "l0a_ksh"], writes=[kA],
               out=eA, in0=eA, in1=ksh[:, sl])
            OP("vector", "tensor_mul", reads=[kA, "l0a_vb"], writes=[kA],
               out=eA, in0=eA, in1=v_bf[:, sl])
            OP("vector", "tensor_mul", reads=[kA, "l0a_gb2"], writes=[kA],
               out=eA, in0=eA, in1=g_bf[:, sl])
            for hh in range(4):
                h = eq * 4 + hh
                OP("vector", "tensor_scalar", reads=[kA, "l0a_dk"], writes=[kM],
                   out=eM[:, hh * 128:(hh + 1) * 128],
                   in0=eA[:, hh * 128:(hh + 1) * 128],
                   scalar1=s["dk"][:, j * 16 + h: j * 16 + h + 1],
                   scalar2=None, op0=ALU.mult)
            csl = crow[0:1, sl]
            ckey = ("l0a_crow", eq)
            pbank = 2 + eq % 2
            OP("tensor", "matmul", reads=["l0a_ltri", kM], writes=[f"ps{pbank}"],
               out=ctx.pb[pbank][:, 0:512], lhsT=s["ltri"], rhs=eM, start=True, stop=False)
            OP("tensor", "matmul", reads=["c_1rowf", ckey], writes=[f"ps{pbank}"],
               out=ctx.pb[pbank][:, 0:512], lhsT=ctx.ones_row_f32, rhs=csl,
               start=False, stop=True)
            pb2 = 4 + eq % 2
            OP("tensor", "matmul", reads=["c_ones", kM], writes=[f"ps{pb2}"],
               out=ctx.pb[pb2][0:1, 0:512], lhsT=ctx.ones_col, rhs=eM,
               start=True, stop=True)
            OP("vector", "tensor_add", reads=[f"ps{pb2}", ckey], writes=[ckey],
               out=csl, in0=csl, in1=ctx.pb[pb2][0:1, 0:512])
            for hh in range(4):
                h = eq * 4 + hh
                OP("vector", "tensor_scalar", reads=[f"ps{pbank}", "l0a_rd"], writes=[kM],
                   out=eM[:, hh * 128:(hh + 1) * 128],
                   in0=ctx.pb[pbank][:, hh * 128:(hh + 1) * 128],
                   scalar1=s["rd"][:, j * 16 + h: j * 16 + h + 1],
                   scalar2=None, op0=ALU.mult)
            OP("vector", "tensor_tensor", reads=[kM, "l0a_qkb"], writes=["l0a_vb"],
               out=v_bf[:, sl], in0=eM, in1=qk_bf[:, sl], op=ALU.mult)
        DMA("sync", ksh[0:1, :], qk_bf[127:128, :], reads=["l0a_qkb", "l0a_ksh"],
            writes=["l0a_ksh"], channel=2)
        for e in range(16):
            transpose128(ctx, a_fm[:, e * 128:(e + 1) * 128], "l0a_afm",
                         v_bf[:, e * 128:(e + 1) * 128], "l0a_vb",
                         BF16, pt_idx=0, pt_slot=e % 8)
        for cb in range(2):
            for e in range(16):
                wt = wtile[wpar % 2]
                wk = ("l0a_wtile", wpar % 2)
                wpar += 1
                DMA("sync", wt, s["ow"][e * 128:(e + 1) * 128, cb * 512:(cb + 1) * 512],
                    reads=[wk, "l0a_owb"], writes=[wk], channel=1)
                OP("tensor", "matmul", reads=[wk, "l0a_afm"],
                   writes=["ps0"],
                   out=ctx.pb[0][:, 0:512],
                   lhsT=a_fm[:, e * 128:(e + 1) * 128], rhs=wt,
                   start=(e == 0), stop=(e == 15))
            OP("vector", "tensor_copy", reads=["ps0"], writes=[OFK[cb]],
               out=of[:, cb * 512:(cb + 1) * 512], in_=ctx.pb[0][:, 0:512])
        mu4, var4 = stats[:, 4:5], stats[:, 5:6]
        emit_ln_stats_tm(ctx, of, OFK, mu4, var4, "l0a_stats", escr, ESK, 1024)
        emit_rsqrt(ctx, var4, var4, "l0a_stats", LN_EPS)
        OP("vector", "scalar_tensor_tensor", reads=["l0a_stats"], writes=["l0a_stats"],
           out=stats[:, 6:7], in0=mu4, scalar=-1.0, in1=var4, op0=ALU.mult, op1=ALU.mult)
        OP("vector", "tensor_scalar", reads=OFK + ["l0a_stats"], writes=OFK,
           out=of, in0=of, scalar1=var4, scalar2=stats[:, 6:7], op0=ALU.mult, op1=ALU.add)
        OP("vector", "tensor_mul", reads=OFK + ["l0a_lngbc"], writes=OFK,
           out=of, in0=of, in1=s["lng_bc"])
        OP("vector", "tensor_add", reads=OFK + ["l0a_lnbbc"], writes=OFK,
           out=of, in0=of, in1=s["lnb_bc"])
        mu5, var5 = stats[:, 4:5], stats[:, 5:6]
        emit_ln_stats_tm(ctx, of, OFK, mu5, var5, "l0a_stats", escr, ESK, 1024)
        emit_rsqrt(ctx, var5, var5, "l0a_stats", LN_EPS)
        OP("vector", "scalar_tensor_tensor", reads=["l0a_stats"], writes=["l0a_stats"],
           out=stats[:, 6:7], in0=mu5, scalar=-1.0, in1=var5, op0=ALU.mult, op1=ALU.mult)
        OP("vector", "tensor_scalar", reads=OFK + ["l0a_stats"], writes=ESK,
           out=escr, in0=of, scalar1=var5, scalar2=stats[:, 6:7], op0=ALU.mult, op1=ALU.add)
        OP("vector", "tensor_add", reads=[xck] + ESK, writes=[xck],
           out=xc_, in0=xc_, in1=escr)
    mk.barrier()
    _sc.close()


"""L0_ffn block emitter: x += LN0(ffn(LN0(x))) with inner double-LN ffn.

ffn(z) = LN2(gelu(LN1(z)@w1q' + b1)) @ w2q' + b2   (ternary-quantized w1,w2)

Derivation (exact algebra, no dropped terms):
  outer LN: mu, var of x;  rstd = (var+eps)^-1/2
  ln1 of x_inner: mu2 = 0 exactly, var2 = var*rstd^2; rstd2 = (var2+eps)^-1/2
  xi = x*A + Bc  with A = rstd*rstd2, Bc = -mu*A      (g1 folded into w1q')
  h = gelu(w1q'^T xi + b1tot),  w1q'[d,f] = tq(w1)[f,d]*g1[d],
      b1tot[f] = b1[f] + sum_d ln1_b[d]*tq(w1)[f,d]
  ln2 stats over f per token: mu2f, rstd2f; A' = rstd2f, B' = -mu2f*rstd2f
  out2[c,t] = A'[t]*(w2q'^T h)[c,t] + B'[t]*v2[c] + cc[c]
    w2q'[f,c] = tq(w2)[c,f]*g2[f], v2[c] = sum_f w2q'[f,c],
    cc[c] = sum_f tq(w2)[c,f]*ln2_b[f] + b2[c]
  x += (out2 - mu3)*rstd3
"""


def emit_l0f_prologue(ctx, din):
    nc, mk = ctx.nc, ctx.mk
    OP, DMA = ctx.OP, ctx.DMA
    s = {}
    ctx.l0f = s
    s["w1q"] = nc.dram_tensor("l0f_w1q", [1024, 4096], BF16, kind="Internal").ap()
    s["w2q"] = nc.dram_tensor("l0f_w2q", [4096, 1024], BF16, kind="Internal").ap()

    s["b1_col"] = ctx.sbuf("l0f_b1c", [128, 32], F32)
    s["g1_col"] = ctx.sbuf("l0f_g1c", [128, 8], F32)
    s["g2_col"] = ctx.sbuf("l0f_g2c", [128, 32], F32)
    s["b2lnb_col"] = ctx.sbuf("l0f_lnbc", [128, 32], F32)
    s["v2_bc"] = ctx.sbuf("l0f_v2bc", [128, 1024], F32)
    s["cc_bc"] = ctx.sbuf("l0f_ccbc", [128, 1024], F32)
    s["b1ln_col"] = ctx.sbuf("l0f_b1lnc", [128, 8], F32)
    DMA("sync", s["b1_col"], din["b1_col"], writes=["l0f_b1c"])
    DMA("sync", s["g2_col"], din["ln2_g_col"], writes=["l0f_g2c"])
    DMA("sync", s["b2lnb_col"], din["ln2_b_col"], writes=["l0f_lnbc"])
    DMA("sync", s["g1_col"], din["ln1_g_col"], writes=["l0f_g1c"])
    DMA("sync", s["b1ln_col"], din["ln1_b_col"], writes=["l0f_b1lnc"])

    _scope = ExitStack()
    def scoped(name, shape, dt):
        return _scope.enter_context(nc.sbuf_tensor(name, shape, dt)).ap()

    win = scoped("l0f_qin", [128, 2048], F32)
    wab = scoped("l0f_qab", [128, 2048], F32)
    wsg = scoped("l0f_qsg", [128, 2048], F32)
    wout = scoped("l0f_qout", [128, 2048], BF16)
    acc = scoped("l0f_qacc", [128, 40], F32)
    mrow = scoped("l0f_qm", [1, 8], F32)
    mcol = scoped("l0f_qmc", [128, 3], F32)

    def quantize(wT_ap, nrows, ncols, out_blob, g_ap, g_key, tagp):
        nch = nrows // 128
        hw = min(ncols, 2048)
        nh = ncols // hw
        W, AB, SG, OU = (t[:, 0:hw] for t in (win, wab, wsg, wout))
        for c in range(nch):
            for hh in range(nh):
                DMA("sync", W, wT_ap[c * 128:(c + 1) * 128, hh * hw:(hh + 1) * hw],
                    reads=["l0f_qinb"], writes=["l0f_qinb"], channel=1)
                OP("vector", "tensor_reduce", reads=["l0f_qinb"], writes=["l0f_qacc"],
                   out=acc[:, c * nh + hh:c * nh + hh + 1], in_=W, axis=AX.X, op=ALU.add,
                   apply_absolute_value=True)
        OP("tensor", "matmul", reads=["l0f_qacc", "c_ones"], writes=["ps0"],
           out=ctx.pb[0][0:1, 0:nch * nh], lhsT=ctx.ones_col, rhs=acc[:, 0:nch * nh],
           start=True, stop=True)
        OP("vector", "reduce_sum", reads=["ps0"], writes=["l0f_qm"],
           out=mrow[:, 0:1], in_=ctx.pb[0][0:1, 0:nch * nh], axis=AX.X)
        OP("vector", "tensor_scalar", reads=["l0f_qm"], writes=["l0f_qm"],
           out=mrow[:, 1:2], in0=mrow[:, 0:1], scalar1=1.0 / (nrows * ncols), scalar2=1e-5,
           op0=ALU.mult, op1=ALU.max)
        OP("vector", "reciprocal", reads=["l0f_qm"], writes=["l0f_qm"],
           out=mrow[:, 2:3], in_=mrow[:, 1:2])
        OP("tensor", "matmul", reads=["l0f_qm", "c_1rowf"], writes=["ps0"],
           out=ctx.pb[0][:, 0:2], lhsT=ctx.ones_row_f32, rhs=mrow[:, 1:3], start=True, stop=True)
        OP("vector", "tensor_copy", reads=["ps0"], writes=["l0f_qmc"],
           out=mcol[:, 0:2], in_=ctx.pb[0][:, 0:2])
        for c in range(nch):
            OP("vector", "tensor_tensor", reads=["l0f_qmc", g_key], writes=["l0f_qmc"],
               out=mcol[:, 2:3], in0=mcol[:, 0:1], in1=g_ap[:, c:c + 1], op=ALU.mult)
            for hh in range(nh):
                DMA("sync", W, wT_ap[c * 128:(c + 1) * 128, hh * hw:(hh + 1) * hw],
                    reads=["l0f_qinb"], writes=["l0f_qinb"], channel=1)
                OP("scalar", "activation", reads=["l0f_qinb"], writes=["l0f_qab"],
                   out=AB, in_=W, func=AF.Abs)
                OP("vector", "tensor_scalar", reads=["l0f_qab", "l0f_qmc"], writes=["l0f_qab"],
                   out=AB, in0=AB, scalar1=mcol[:, 1:2], scalar2=0.5, op0=ALU.mult, op1=ALU.is_ge)
                OP("scalar", "activation", reads=["l0f_qinb"], writes=["l0f_qsg"],
                   out=SG, in_=W, func=AF.Sign)
                OP("vector", "scalar_tensor_tensor",
                   reads=["l0f_qab", "l0f_qsg", "l0f_qmc"], writes=["l0f_qout"],
                   out=OU, in0=AB, scalar=mcol[:, 2:3], in1=SG, op0=ALU.mult, op1=ALU.mult)
                DMA("sync", out_blob[c * 128:(c + 1) * 128, hh * hw:(hh + 1) * hw], OU,
                    reads=["l0f_qout"], writes=[f"{tagp}_blob"], channel=2)

    quantize(din["w1T"], 1024, 4096, s["w1q"], s["g1_col"], "l0f_g1c", "l0fq1")
    quantize(din["w2T"], 4096, 1024, s["w2q"], s["g2_col"], "l0f_g2c", "l0fq2")

    bg1 = scoped("l0f_bg1", [128, 8], F32)
    bg2 = scoped("l0f_bg2", [128, 32], F32)
    bg1_bf = scoped("l0f_bg1b", [128, 8], BF16)
    bg2_bf = scoped("l0f_bg2b", [128, 32], BF16)
    OP("vector", "reciprocal", reads=["l0f_g1c"], writes=["l0f_bg1"], out=bg1, in_=s["g1_col"])
    OP("vector", "tensor_mul", reads=["l0f_bg1", "l0f_b1lnc"], writes=["l0f_bg1"],
       out=bg1, in0=bg1, in1=s["b1ln_col"])
    OP("vector", "reciprocal", reads=["l0f_g2c"], writes=["l0f_bg2"], out=bg2, in_=s["g2_col"])
    OP("vector", "tensor_mul", reads=["l0f_bg2", "l0f_lnbc"], writes=["l0f_bg2"],
       out=bg2, in0=bg2, in1=s["b2lnb_col"])
    OP("vector", "tensor_copy", reads=["l0f_bg1"], writes=["l0f_bg1b"], out=bg1_bf, in_=bg1)
    OP("vector", "tensor_copy", reads=["l0f_bg2"], writes=["l0f_bg2b"], out=bg2_bf, in_=bg2)

    r1row = scoped("l0f_r1row", [1, 4096], F32)
    wqb1 = win.bitcast(BF16)
    OP("vector", "memset", writes=["l0f_r1row"], ap=r1row, constant=0.0)
    for c in range(8):
        DMA("sync", wqb1, s["w1q"][c * 128:(c + 1) * 128, :],
            reads=["l0f_qinb", "l0fq1_blob"], writes=["l0f_qinb"], channel=1)
        for g in range(8):
            bank = g % 2
            OP("tensor", "matmul", reads=["l0f_qinb", "l0f_bg1b"], writes=[f"ps{bank}"],
               out=ctx.pb[bank][0:1, 0:512], lhsT=bg1_bf[:, c:c + 1],
               rhs=wqb1[:, g * 512:(g + 1) * 512], start=True, stop=True)
            OP("vector", "tensor_add", reads=[f"ps{bank}", "l0f_r1row"], writes=["l0f_r1row"],
               out=r1row[:, g * 512:(g + 1) * 512], in0=r1row[:, g * 512:(g + 1) * 512],
               in1=ctx.pb[bank][0:1, 0:512])
    for fc in range(32):
        OP("tensor", "transpose", reads=["l0f_r1row", "c_idf"], writes=["ps5"],
           out=ctx.pb[5][:, 0:1], in_=r1row[0:1, fc * 128:(fc + 1) * 128],
           identity=ctx.ident_f32[0:1, 0:1])
        OP("vector", "tensor_add", reads=["ps5", "l0f_b1c"], writes=["l0f_b1c"],
           out=s["b1_col"][:, fc:fc + 1], in0=s["b1_col"][:, fc:fc + 1], in1=ctx.pb[5][:, 0:1])

    vrow = scoped("l0f_vrow", [1, 1024], F32)
    crow = scoped("l0f_crow", [1, 1024], F32)
    wqb = wab.bitcast(BF16)[:, 0:2048][:, 0:1024]
    for c in range(32):
        DMA("sync", wqb, s["w2q"][c * 128:(c + 1) * 128, :],
            reads=["l0f_qab", "l0fq2_blob"], writes=["l0f_qab"], channel=1)
        for g in range(2):
            OP("tensor", "matmul", reads=["l0f_qab", "c_onesb"], writes=[f"ps{g}"],
               out=ctx.pb[g][0:1, 0:512], lhsT=ctx.ones_col_bf,
               rhs=wqb[:, g * 512:(g + 1) * 512], start=(c == 0), stop=(c == 31))
            OP("tensor", "matmul", reads=["l0f_qab", "l0f_bg2b"], writes=[f"ps{2+g}"],
               out=ctx.pb[2 + g][0:1, 0:512], lhsT=bg2_bf[:, c:c + 1],
               rhs=wqb[:, g * 512:(g + 1) * 512], start=(c == 0), stop=(c == 31))
    for g in range(2):
        OP("vector", "tensor_copy", reads=[f"ps{g}"], writes=["l0f_vrow"],
           out=vrow[:, g * 512:(g + 1) * 512], in_=ctx.pb[g][0:1, 0:512])
        OP("vector", "tensor_copy", reads=[f"ps{2+g}"], writes=["l0f_crow"],
           out=crow[:, g * 512:(g + 1) * 512], in_=ctx.pb[2 + g][0:1, 0:512])
    b2r = wsg[0:1, 0:1024]
    DMA("sync", b2r, din["b2_row"], writes=["l0f_qsg"])
    OP("vector", "tensor_add", reads=["l0f_crow", "l0f_qsg"], writes=["l0f_crow"],
       out=crow, in0=crow, in1=b2r)
    for g in range(2):
        OP("tensor", "matmul", reads=["l0f_vrow", "c_1rowf"], writes=["ps0"],
           out=ctx.pb[0][:, 0:512], lhsT=ctx.ones_row_f32,
           rhs=vrow[:, g * 512:(g + 1) * 512], start=True, stop=True)
        OP("vector", "tensor_copy", reads=["ps0"], writes=["l0f_v2bc"],
           out=s["v2_bc"][:, g * 512:(g + 1) * 512], in_=ctx.pb[0][:, 0:512])
        OP("tensor", "matmul", reads=["l0f_crow", "c_1rowf"], writes=["ps1"],
           out=ctx.pb[1][:, 0:512], lhsT=ctx.ones_row_f32,
           rhs=crow[:, g * 512:(g + 1) * 512], start=True, stop=True)
        OP("vector", "tensor_copy", reads=["ps1"], writes=["l0f_ccbc"],
           out=s["cc_bc"][:, g * 512:(g + 1) * 512], in_=ctx.pb[1][:, 0:512])
    mk.barrier()
    _scope.close()


def emit_l0f(ctx, x_sb, xkey, debug_xi=None):
    nc, mk = ctx.nc, ctx.mk
    OP, DMA = ctx.OP, ctx.DMA
    s = ctx.l0f
    NTILE = ctx.NTILE

    sscr = ctx.sbuf("l0f_sscr", [128, 1024], F32)
    share = ctx.sbuf("l0f_share", [128, 8192], BF16)
    xi_tm = share[:, 0:4096]
    xi_fm = share[:, 4096:8192]
    o2t = share.bitcast(F32)[:, 0:4096]
    h_bf = ctx.sbuf("l0f_h", [128, 32 * 512], BF16)
    hsq = share[:, 0:1024]
    o2f = sscr
    w1b = [ctx.sbuf(f"l0f_w1b{i}", [128, 8 * 512], BF16) for i in range(2)]
    stats = ctx.sbuf("l0f_st", [128, 16], F32)
    scol8 = ctx.sbuf("l0f_scol8", [128, 8], F32)

    for it in range(NTILE):
        js = [4 * it + u for u in range(4)]
        for u, j in enumerate(js):
            xc = x_sb[:, j * 1024:(j + 1) * 1024]
            xck = (xkey, j)
            mu, var = stats[:, 0:1], stats[:, 1:2]
            emit_ln_stats_tm(ctx, xc, xck, mu, var, "l0f_st", sscr, "l0f_sscr", 1024)
            rstd, rstd2, A, Bc = stats[:, 2:3], stats[:, 3:4], stats[:, 4:5], stats[:, 5:6]
            OP("vector", "tensor_copy", reads=["l0f_st"], writes=["l0f_st"], out=rstd, in_=var)
            emit_rsqrt(ctx, rstd, rstd, "l0f_st", LN_EPS)
            OP("vector", "tensor_mul", reads=["l0f_st"], writes=["l0f_st"],
               out=rstd2, in0=rstd, in1=rstd)
            OP("vector", "tensor_mul", reads=["l0f_st"], writes=["l0f_st"],
               out=rstd2, in0=rstd2, in1=var)
            emit_rsqrt(ctx, rstd2, rstd2, "l0f_st", LN_EPS)
            OP("vector", "tensor_mul", reads=["l0f_st"], writes=["l0f_st"],
               out=A, in0=rstd, in1=rstd2)
            OP("vector", "scalar_tensor_tensor", reads=["l0f_st"], writes=["l0f_st"],
               out=Bc, in0=mu, scalar=-1.0, in1=A, op0=ALU.mult, op1=ALU.mult)
            OP("vector", "tensor_scalar", reads=[xck, "l0f_st"], writes=["l0f_shr"],
               out=xi_tm[:, u * 1024:(u + 1) * 1024], in0=xc, scalar1=A, scalar2=Bc,
               op0=ALU.mult, op1=ALU.add)
            for c in range(8):
                transpose128(ctx, xi_fm[:, c * 512 + u * 128: c * 512 + (u + 1) * 128],
                             "l0f_shr", xi_tm[:, u * 1024 + c * 128: u * 1024 + (c + 1) * 128],
                             "l0f_shr", BF16, pt_idx=0, pt_slot=(u * 8 + c) % 8)
        if debug_xi is not None and it == 0:
            DMA("sync", debug_xi[0], xi_fm, reads=["l0f_shr"], channel=3)
            DMA("sync", debug_xi[1], xi_tm, reads=["l0f_shr"], channel=3)
        for fb in range(8):
            wk = ("l0f_w1b", fb % 2)
            for c in range(8):
                DMA("sync", w1b[fb % 2][:, c * 512:(c + 1) * 512],
                    s["w1q"][c * 128:(c + 1) * 128, fb * 512:(fb + 1) * 512],
                    reads=[wk, "l0fq1_blob"], writes=[wk], channel=1 + fb % 2)
            for fs in range(4):
                fc = fb * 4 + fs
                bank = fc % 2
                for c in range(8):
                    OP("tensor", "matmul", reads=[wk, "l0f_shr"], writes=[f"ps{bank}"],
                       out=ctx.pb[bank][:, 0:512],
                       lhsT=w1b[fb % 2][:, c * 512 + fs * 128: c * 512 + (fs + 1) * 128],
                       rhs=xi_fm[:, c * 512:(c + 1) * 512], start=(c == 0), stop=(c == 7))
                OP("scalar", "activation", reads=[f"ps{bank}", "l0f_b1c"], writes=["l0f_h"],
                   out=h_bf[:, fc * 512:(fc + 1) * 512], in_=ctx.pb[bank][:, 0:512],
                   func=AF.Gelu, bias=s["b1_col"][:, fc:fc + 1])
                OP("scalar", "activation", reads=["l0f_h"], writes=["l0f_shr"],
                   out=hsq[:, (fc % 2) * 512:(fc % 2 + 1) * 512],
                   in_=h_bf[:, fc * 512:(fc + 1) * 512], func=AF.Square)
                OP("tensor", "matmul", reads=["l0f_h", "c_onesb"], writes=["ps2"],
                   out=ctx.pb[2][0:1, 0:512], lhsT=ctx.ones_col_bf,
                   rhs=h_bf[:, fc * 512:(fc + 1) * 512], start=(fc == 0), stop=(fc == 31))
                OP("tensor", "matmul", reads=["l0f_shr", "c_onesb"], writes=["ps3"],
                   out=ctx.pb[3][0:1, 0:512], lhsT=ctx.ones_col_bf,
                   rhs=hsq[:, (fc % 2) * 512:(fc % 2 + 1) * 512], start=(fc == 0), stop=(fc == 31))
        mu2f, r2f = sscr[0:1, 0:512], sscr[0:1, 512:1024]
        OP("vector", "tensor_scalar", reads=["ps2"], writes=["l0f_sscr"],
           out=mu2f, in0=ctx.pb[2][0:1, 0:512], scalar1=1.0 / 4096, scalar2=None, op0=ALU.mult)
        OP("vector", "tensor_scalar", reads=["ps3"], writes=["l0f_sscr"],
           out=r2f, in0=ctx.pb[3][0:1, 0:512], scalar1=1.0 / 4096, scalar2=None, op0=ALU.mult)
        OP("vector", "scalar_tensor_tensor", reads=["l0f_sscr"], writes=["ps2"],
           out=ctx.pb[2][0:1, 0:512], in0=mu2f, scalar=-1.0, in1=mu2f,
           op0=ALU.mult, op1=ALU.mult)
        OP("vector", "tensor_add", reads=["l0f_sscr", "ps2"], writes=["l0f_sscr"],
           out=r2f, in0=r2f, in1=ctx.pb[2][0:1, 0:512])
        emit_rsqrt(ctx, r2f, r2f, "l0f_sscr", LN_EPS)
        OP("vector", "scalar_tensor_tensor", reads=["l0f_sscr"], writes=["l0f_sscr"],
           out=mu2f, in0=mu2f, scalar=-1.0, in1=r2f, op0=ALU.mult, op1=ALU.mult)
        for u in range(4):
            OP("tensor", "transpose", reads=["l0f_sscr", "c_idf"], writes=["ps5"],
               out=ctx.pb[5][:, 2 * u:2 * u + 1],
               in_=sscr[0:1, 512 + u * 128: 512 + (u + 1) * 128],
               identity=ctx.ident_f32[0:1, 0:1])
            OP("tensor", "transpose", reads=["l0f_sscr", "c_idf"], writes=["ps5"],
               out=ctx.pb[5][:, 2 * u + 1:2 * u + 2],
               in_=sscr[0:1, u * 128:(u + 1) * 128],
               identity=ctx.ident_f32[0:1, 0:1])
        OP("vector", "tensor_copy", reads=["ps5"], writes=["l0f_scol8"],
           out=scol8, in_=ctx.pb[5][:, 0:8])
        for half in range(2):
            for fb in range(8):
                wk = ("l0f_w1b", fb % 2)
                for fs in range(4):
                    DMA("sync", w1b[fb % 2][:, fs * 1024 + half * 512: fs * 1024 + (half + 1) * 512],
                        s["w2q"][fb * 512 + fs * 128: fb * 512 + (fs + 1) * 128,
                                 half * 512:(half + 1) * 512],
                        reads=[wk, "l0fq2_blob"], writes=[wk], channel=1 + fb % 2)
                for cs in range(4):
                    for fs in range(4):
                        fc = fb * 4 + fs
                        OP("tensor", "matmul", reads=[wk, "l0f_h"], writes=[f"ps{cs}"],
                           out=ctx.pb[cs][:, 0:512],
                           lhsT=w1b[fb % 2][:, fs * 1024 + half * 512 + cs * 128:
                                            fs * 1024 + half * 512 + (cs + 1) * 128],
                           rhs=h_bf[:, fc * 512:(fc + 1) * 512],
                           start=(fb == 0 and fs == 0), stop=(fb == 7 and fs == 3))
            for cs in range(4):
                c = half * 4 + cs
                slot = c % 2
                OP("vector", "tensor_copy", reads=[f"ps{cs}"], writes=["l0f_sscr"],
                   out=o2f[:, slot * 512:(slot + 1) * 512], in_=ctx.pb[cs][:, 0:512])
                for u in range(4):
                    transpose128(ctx, o2t[:, u * 1024 + c * 128: u * 1024 + (c + 1) * 128],
                                 "l0f_shr", o2f[:, slot * 512 + u * 128: slot * 512 + (u + 1) * 128],
                                 "l0f_sscr", F32, pt_slot=(c * 4 + u) % 4)
        for u, j in enumerate(js):
            Acol, Bcol = scol8[:, 2 * u:2 * u + 1], scol8[:, 2 * u + 1:2 * u + 2]
            ot = o2t[:, u * 1024:(u + 1) * 1024]
            OP("vector", "scalar_tensor_tensor",
               reads=["l0f_v2bc", "l0f_ccbc", "l0f_scol8"], writes=["l0f_sscr"],
               out=sscr, in0=s["v2_bc"], scalar=Bcol, in1=s["cc_bc"], op0=ALU.mult, op1=ALU.add)
            OP("vector", "scalar_tensor_tensor",
               reads=["l0f_shr", "l0f_sscr", "l0f_scol8"], writes=["l0f_shr"],
               out=ot, in0=ot, scalar=Acol, in1=sscr, op0=ALU.mult, op1=ALU.add)
            mu3, var3 = stats[:, 6:7], stats[:, 7:8]
            emit_ln_stats_tm(ctx, ot, "l0f_shr", mu3, var3, "l0f_st", sscr, "l0f_sscr", 1024)
            emit_rsqrt(ctx, var3, var3, "l0f_st", LN_EPS)
            OP("vector", "scalar_tensor_tensor", reads=["l0f_st"], writes=["l0f_st"],
               out=stats[:, 8:9], in0=mu3, scalar=-1.0, in1=var3, op0=ALU.mult, op1=ALU.mult)
            xc = x_sb[:, j * 1024:(j + 1) * 1024]
            xck = (xkey, j)
            OP("vector", "tensor_scalar", reads=["l0f_shr", "l0f_st"], writes=["l0f_sscr"],
               out=sscr, in0=ot, scalar1=var3, scalar2=stats[:, 8:9], op0=ALU.mult, op1=ALU.add)
            OP("vector", "tensor_add", reads=[xck, "l0f_sscr"], writes=[xck],
               out=xc, in0=xc, in1=sscr)

"""L1_attn block: x += LN0(attn(LN0(x))) - 16 heads x 64, ALiBi, softcap 30,
causal softmax, out proj + bo.

 - K~ blob [16, 68, T]: rows 0..63 = K head slab (f-major), rows 64/65 =
   bf16 hi/lo split of s_h*tk (alibi), rows 66/67 = ones.
   Q~ tile [68, 512]: rows 0..63 = Q, rows 64/65 = ones, rows 66/67 =
   bf16 hi/lo of -s_h*tq.  S^T = K~^T Q~ includes exact-ish alibi.
 - P^T = exp(30*tanh(S^T/240) - 30): softcap bounds scores, fixed max.
 - V^ blob [T/128, 128, 16*65]: per head 64 V cols + ones col; PV matmul
   accumulates ctx^T [65, tq] in PSUM; row 64 = softmax denominator l.
 - ctx * (1/l) via transpose-dance reciprocal; Wo proj + bo; outer LN+res.
"""


def emit_l1a_prologue(ctx, din):
    nc, mk = ctx.nc, ctx.mk
    OP, DMA = ctx.OP, ctx.DMA
    T = ctx.T
    s = {}
    ctx.l1a = s
    s["kblob"] = nc.dram_tensor("l1a_kblob", [16, 68, T], BF16, kind="Internal").ap()
    s["vblob"] = nc.dram_tensor("l1a_vblob", [T // 128, 128, 16 * 65], BF16, kind="Internal").ap()
    s["xin_sp"] = nc.dram_tensor("l1a_xinsp", [8, 128, T], BF16, kind="Internal").ap()
    for w in ("q", "k", "v", "o"):
        s[f"w{w}_blk"] = nc.dram_tensor(f"l1a_w{w}blk", [8, 128, 1024], BF16, kind="Internal").ap()
    s["aq_rows"] = din["aq_rows"]
    s["bo_col"] = ctx.sbuf("l1a_boc", [128, 8], F32)
    DMA("sync", s["bo_col"], din["bo_col"], writes=["l1a_boc"])
    s["masks"] = ctx.sbuf("l1a_masks_sb", [128, 4 * 512], BF16)
    DMA("sync", s["masks"], din["l1a_masks"], writes=["l1a_masks"])

    _scope = ExitStack()
    def scoped(name, shape, dt):
        return _scope.enter_context(nc.sbuf_tensor(name, shape, dt)).ap()

    win = scoped("l1a_qin", [128, 1024], F32)
    wbf = scoped("l1a_qbf", [128, 1024], BF16)
    for w in ("q", "k", "v", "o"):
        src = din[f"W{w}T"]
        blob = s[f"w{w}_blk"]
        for c in range(8):
            DMA("sync", win, src[c * 128:(c + 1) * 128, :],
                reads=["l1a_qin"], writes=["l1a_qin"], channel=1)
            OP("vector", "tensor_copy", reads=["l1a_qin"], writes=["l1a_qbf"],
               out=wbf, in_=win)
            for blk in range(8):
                DMA("sync", blob[blk][:, c * 128:(c + 1) * 128],
                    wbf[:, blk * 128:(blk + 1) * 128],
                    reads=["l1a_qbf"], writes=[f"l1a_{w}blob"], channel=2)
    augk = scoped("l1a_augk", [32, T], BF16)
    ones_t = scoped("l1a_onest", [1, T], BF16)
    DMA("sync", augk, din["ak_rows"], writes=["l1a_augk"], channel=1)
    OP("vector", "memset", writes=["l1a_onest"], ap=ones_t, constant=1.0)
    for h in range(16):
        DMA("sync", s["kblob"][h, 66:68, :], augk[2 * h:2 * h + 2, :],
            reads=["l1a_augk"], writes=["l1a_kblob_aug"], channel=2)
        DMA("sync", s["kblob"][h, 64:65, :], ones_t,
            reads=["l1a_onest"], writes=["l1a_kblob_aug"], channel=2)
        DMA("sync", s["kblob"][h, 65:66, :], ones_t,
            reads=["l1a_onest"], writes=["l1a_kblob_aug"], channel=2)
    mk.barrier()
    _scope.close()


def emit_l1a(ctx, x_sb, xkey):
    nc, mk = ctx.nc, ctx.mk
    OP, DMA = ctx.OP, ctx.DMA
    s = ctx.l1a
    T, NTILE = ctx.T, ctx.NTILE
    SC = 30.0

    _scA = ExitStack()
    def scA(name, shape, dt):
        return _scA.enter_context(nc.sbuf_tensor(name, shape, dt)).ap()

    xin_tm = scA("l1a_xintm", [128, 1024], BF16)
    xin_fm = scA("l1a_xinfm", [128, 8 * 512], BF16)
    wkv = scA("l1a_wkv", [128, 16 * 1024], BF16)
    kv_sb = scA("l1a_kvsb", [128, 512], BF16)
    vf_sb = scA("l1a_vfsb", [128, 512], BF16)
    vt_sb = scA("l1a_vtsb", [128, 4 * 1040], BF16)
    statsA = scA("l1a_stA", [128, 8], F32)
    sscrA = scA("l1a_sscrA", [128, 1024], F32)

    OP("vector", "memset", writes=["l1a_vtsb"], ap=vt_sb, constant=1.0)
    for o in range(8):
        DMA("sync", wkv[:, o * 2048: o * 2048 + 1024], s["wk_blk"][o],
            reads=["l1a_wkv"], writes=["l1a_wkv"], channel=1 + o % 2)
        DMA("sync", wkv[:, o * 2048 + 1024: o * 2048 + 2048], s["wv_blk"][o],
            reads=["l1a_wkv"], writes=["l1a_wkv"], channel=1 + o % 2)

    for it in range(NTILE):
        for u in range(4):
            j = 4 * it + u
            xc = x_sb[:, j * 1024:(j + 1) * 1024]
            xck = (xkey, j)
            mu, var = statsA[:, 0:1], statsA[:, 1:2]
            emit_ln_stats_tm(ctx, xc, xck, mu, var, "l1a_st", sscrA, "l1a_sscr", 1024)
            emit_rsqrt(ctx, var, var, "l1a_st", LN_EPS)
            OP("vector", "scalar_tensor_tensor", reads=["l1a_st"], writes=["l1a_st"],
               out=statsA[:, 2:3], in0=mu, scalar=-1.0, in1=var, op0=ALU.mult, op1=ALU.mult)
            OP("vector", "tensor_scalar", reads=[xck, "l1a_st"], writes=["l1a_xintm"],
               out=xin_tm, in0=xc, scalar1=var, scalar2=statsA[:, 2:3], op0=ALU.mult, op1=ALU.add)
            for c in range(8):
                transpose128(ctx, xin_fm[:, c * 512 + u * 128: c * 512 + (u + 1) * 128],
                             "l1a_xinfm", xin_tm[:, c * 128:(c + 1) * 128],
                             "l1a_xintm", BF16, pt_idx=0, pt_slot=c % 8)
        for c in range(8):
            DMA("sync", s["xin_sp"][c][:, it * 512:(it + 1) * 512],
                xin_fm[:, c * 512:(c + 1) * 512],
                reads=["l1a_xinfm"], writes=["l1a_xinspill"], channel=3)
        for o in range(8):
            wk0 = o * 2048
            for c in range(8):
                OP("tensor", "matmul", reads=["l1a_wkv", "l1a_xinfm"], writes=["ps2"],
                   out=ctx.pb[2][:, 0:512], lhsT=wkv[:, wk0 + c * 128: wk0 + (c + 1) * 128],
                   rhs=xin_fm[:, c * 512:(c + 1) * 512], start=(c == 0), stop=(c == 7))
            for c in range(8):
                OP("tensor", "matmul", reads=["l1a_wkv", "l1a_xinfm"], writes=["ps3"],
                   out=ctx.pb[3][:, 0:512],
                   lhsT=wkv[:, wk0 + 1024 + c * 128: wk0 + 1024 + (c + 1) * 128],
                   rhs=xin_fm[:, c * 512:(c + 1) * 512], start=(c == 0), stop=(c == 7))
            OP("vector", "tensor_copy", reads=["ps2"], writes=["l1a_kvsb"],
               out=kv_sb, in_=ctx.pb[2][:, 0:512])
            DMA("sync", s["kblob"][2 * o, 0:64, it * 512:(it + 1) * 512], kv_sb[0:64, :],
                reads=["l1a_kvsb"], writes=["l1a_kblob"], channel=3)
            DMA("sync", s["kblob"][2 * o + 1, 0:64, it * 512:(it + 1) * 512], kv_sb[64:128, :],
                reads=["l1a_kvsb"], writes=["l1a_kblob"], channel=3)
            OP("vector", "tensor_copy", reads=["ps3"], writes=["l1a_vfsb"],
               out=vf_sb, in_=ctx.pb[3][:, 0:512])
            for u in range(4):
                stage = ctx.pt[1][:, (o % 4 + 4 * (u % 2)) * 128:(o % 4 + 4 * (u % 2) + 1) * 128]
                OP("tensor", "transpose", reads=["l1a_vfsb", "c_idb"], writes=["pst1"],
                   out=stage, in_=vf_sb[:, u * 128:(u + 1) * 128], identity=ctx.ident_bf16)
                base = u * 1040
                OP("vector", "tensor_copy", reads=["pst1"], writes=["l1a_vtsb"],
                   out=vt_sb[:, base + 65 * (2 * o): base + 65 * (2 * o) + 64],
                   in_=stage[:, 0:64])
                OP("vector", "tensor_copy", reads=["pst1"], writes=["l1a_vtsb"],
                   out=vt_sb[:, base + 65 * (2 * o + 1): base + 65 * (2 * o + 1) + 64],
                   in_=stage[:, 64:128])
        for u in range(4):
            DMA("sync", s["vblob"][4 * it + u], vt_sb[:, u * 1040: u * 1040 + 1040],
                reads=["l1a_vtsb"], writes=["l1a_vblob"], channel=3)
    mk.barrier()
    _scA.close()

    _scB = ExitStack()
    def scB(name, shape, dt):
        return _scB.enter_context(nc.sbuf_tensor(name, shape, dt)).ap()

    xin_fm2 = scB("l1a_xfm2", [128, 8 * 512], BF16)
    wq_sb = [scB(f"l1a_wqsb{i}", [128, 1024], BF16) for i in range(2)]
    kt = [scB(f"l1a_kt{i}", [68, T], BF16) for i in range(2)]
    vt = [scB(f"l1a_vt{i}", [128, 1040], BF16) for i in range(2)]
    qt = [scB(f"l1a_qt{i}", [128, 512], BF16) for i in range(2)]
    Ts = scB("l1a_Ts", [128, 2 * 512], F32)
    Pt = scB("l1a_Pt", [128, 4 * 512], BF16)
    ctx_nrm = scB("l1a_ctxn", [128, 8 * 512], BF16)
    lrow = Ts[0:1, 0:512]
    rlrow = Ts[0:1, 512:1024]
    lcol = scB("l1a_lcol", [128, 4], F32)
    aqr = scB("l1a_aqr", [2, 512], BF16)
    of_sb = scB("l1a_ofsb", [128, 1024], F32)
    ot = scB("l1a_ot", [128, 4096], F32)
    wo_sb = [scB("l1a_wosb0", [128, 1024], BF16)] * 2
    statsB = scB("l1a_stB", [128, 8], F32)

    ctx.constf(-SC)
    for it in range(NTILE):
        nkc = 4 * it + 4
        for c in range(8):
            DMA("sync", xin_fm2[:, c * 512:(c + 1) * 512],
                s["xin_sp"][c][:, it * 512:(it + 1) * 512],
                reads=["l1a_xfm2", "l1a_xinspill"], writes=["l1a_xfm2"], channel=1)
        for g in range(8):
            wqk = ("l1a_wqsb", g % 2)
            DMA("sync", wq_sb[g % 2], s["wq_blk"][g], reads=[wqk, "l1a_qblob"],
                writes=[wqk], channel=1 + g % 2)
            for c in range(8):
                OP("tensor", "matmul", reads=[wqk, "l1a_xfm2"], writes=["ps4"],
                   out=ctx.pb[4][:, 0:512], lhsT=wq_sb[g % 2][:, c * 128:(c + 1) * 128],
                   rhs=xin_fm2[:, c * 512:(c + 1) * 512], start=(c == 0), stop=(c == 7))
            for hh in range(2):
                h = 2 * g + hh
                qk = ("l1a_qt", hh)
                OP("vector", "tensor_copy", reads=["ps4"], writes=[qk],
                   out=qt[hh][0:64, :], in_=ctx.pb[4][hh * 64:(hh + 1) * 64, 0:512])
                OP("vector", "memset", writes=[qk], ap=qt[hh][64:68, :], constant=1.0)
                DMA("sync", aqr, s["aq_rows"][2 * h:2 * h + 2, it * 512:(it + 1) * 512],
                    reads=["l1a_aqr"], writes=["l1a_aqr"], channel=2)
                OP("vector", "tensor_copy", reads=["l1a_aqr"], writes=[qk],
                   out=qt[hh][64:66, :], in_=aqr)
                DMA("sync", kt[hh][:, 0:512 * (it + 1)], s["kblob"][h][:, 0:512 * (it + 1)],
                    reads=[("l1a_kt", hh), "l1a_kblob", "l1a_kblob_aug"],
                    writes=[("l1a_kt", hh)], channel=1 + hh)
            for kc in range(nkc):
                DMA("sync", vt[kc % 2], s["vblob"][kc],
                    reads=[("l1a_vt", kc % 2), "l1a_vblob"], writes=[("l1a_vt", kc % 2)],
                    channel=3)
                for hh in range(2):
                    h = 2 * g + hh
                    sb = 2 + 2 * (kc % 2) + hh
                    tslot = 2 * (kc % 2) + hh
                    ts2 = hh
                    OP("tensor", "matmul",
                       reads=[("l1a_kt", hh), ("l1a_qt", hh)], writes=[f"ps{sb}"],
                       out=ctx.pb[sb][:, 0:512], lhsT=kt[hh][:, kc * 128:(kc + 1) * 128],
                       rhs=qt[hh][0:68, :], start=True, stop=True)
                    OP("scalar", "activation", reads=[f"ps{sb}"], writes=["l1a_Ts"],
                       out=Ts[:, ts2 * 512:(ts2 + 1) * 512], in_=ctx.pb[sb][:, 0:512],
                       func=AF.Tanh, scale=1.0 / (8.0 * SC))
                    OP("scalar", "activation", reads=["l1a_Ts", "c_pool"], writes=["l1a_Pt"],
                       out=Pt[:, tslot * 512:(tslot + 1) * 512],
                       in_=Ts[:, ts2 * 512:(ts2 + 1) * 512],
                       func=AF.Exp, scale=SC, bias=-SC)
                    if kc >= 4 * it:
                        d = kc - 4 * it
                        OP("vector", "tensor_tensor", reads=["l1a_Pt", "l1a_masks"],
                           writes=["l1a_Pt"],
                           out=Pt[:, tslot * 512:(tslot + 1) * 512],
                           in0=Pt[:, tslot * 512:(tslot + 1) * 512],
                           in1=s["masks"][:, d * 512:(d + 1) * 512], op=ALU.mult)
                    OP("tensor", "matmul",
                       reads=[("l1a_vt", kc % 2), "l1a_Pt"], writes=[f"ps{hh}"],
                       out=ctx.pb[hh][0:65, 0:512],
                       lhsT=vt[kc % 2][:, 65 * h: 65 * h + 65],
                       rhs=Pt[:, tslot * 512:(tslot + 1) * 512],
                       start=(kc == 0), stop=(kc == nkc - 1))
            for hh in range(2):
                h = 2 * g + hh
                OP("vector", "tensor_copy", reads=[f"ps{hh}"], writes=["l1a_Ts"],
                   out=lrow, in_=ctx.pb[hh][64:65, 0:512])
                for q4 in range(4):
                    OP("tensor", "transpose", reads=["l1a_Ts", "c_idf"], writes=["ps5"],
                       out=ctx.pb[5][:, q4:q4 + 1], in_=lrow[0:1, q4 * 128:(q4 + 1) * 128],
                       identity=ctx.ident_f32[0:1, 0:1])
                OP("vector", "tensor_copy", reads=["ps5"], writes=["l1a_lcol"],
                   out=lcol, in_=ctx.pb[5][:, 0:4])
                OP("vector", "reciprocal", reads=["l1a_lcol"], writes=["l1a_lcol"],
                   out=lcol, in_=lcol)
                for q4 in range(4):
                    OP("tensor", "transpose", reads=["l1a_lcol", "c_idf"], writes=["ps5"],
                       out=ctx.pb[5][0:1, q4 * 128:(q4 + 1) * 128],
                       in_=lcol[:, q4:q4 + 1], identity=ctx.ident_f32)
                OP("vector", "tensor_copy", reads=["ps5"], writes=["l1a_Ts"],
                   out=rlrow, in_=ctx.pb[5][0:1, 0:512])
                OP("tensor", "matmul", reads=["l1a_Ts", "c_1rowf"], writes=["ps5"],
                   out=ctx.pb[5][0:64, 0:512], lhsT=ctx.ones_row_f32[:, 0:64], rhs=rlrow,
                   start=True, stop=True)
                OP("vector", "tensor_copy", reads=["ps5"], writes=["l1a_of"],
                   out=of_sb[0:64, 0:512], in_=ctx.pb[5][0:64, 0:512])
                OP("vector", "tensor_tensor", reads=[f"ps{hh}", "l1a_of"],
                   writes=["l1a_ctxn"],
                   out=ctx_nrm[hh * 64: hh * 64 + 64, g * 512:(g + 1) * 512],
                   in0=ctx.pb[hh][0:64, 0:512], in1=of_sb[0:64, 0:512], op=ALU.mult)
        for c in range(8):
            wok = "l1a_wosb"
            DMA("sync", wo_sb[c % 2], s["wo_blk"][c], reads=[wok, "l1a_oblob"],
                writes=[wok], channel=1 + c % 2)
            for e in range(8):
                OP("tensor", "matmul", reads=[wok, "l1a_ctxn"], writes=["ps4"],
                   out=ctx.pb[4][:, 0:512], lhsT=wo_sb[c % 2][:, e * 128:(e + 1) * 128],
                   rhs=ctx_nrm[:, e * 512:(e + 1) * 512], start=(e == 0), stop=(e == 7))
            slot = c % 2
            OP("vector", "tensor_scalar", reads=["ps4", "l1a_boc"], writes=["l1a_of"],
               out=of_sb[:, slot * 512:(slot + 1) * 512], in0=ctx.pb[4][:, 0:512],
               scalar1=s["bo_col"][:, c:c + 1], scalar2=None, op0=ALU.add)
            for u in range(4):
                transpose128(ctx, ot[:, u * 1024 + c * 128: u * 1024 + (c + 1) * 128],
                             "l1a_ot", of_sb[:, slot * 512 + u * 128: slot * 512 + (u + 1) * 128],
                             "l1a_of", F32, pt_slot=(c * 4 + u) % 4)
        for u in range(4):
            j = 4 * it + u
            otc = ot[:, u * 1024:(u + 1) * 1024]
            mu3, var3 = statsB[:, 0:1], statsB[:, 1:2]
            emit_ln_stats_tm(ctx, otc, "l1a_ot", mu3, var3, "l1a_stB", of_sb, "l1a_of", 1024)
            emit_rsqrt(ctx, var3, var3, "l1a_stB", LN_EPS)
            OP("vector", "scalar_tensor_tensor", reads=["l1a_stB"], writes=["l1a_stB"],
               out=statsB[:, 2:3], in0=mu3, scalar=-1.0, in1=var3, op0=ALU.mult, op1=ALU.mult)
            xc = x_sb[:, j * 1024:(j + 1) * 1024]
            xck = (xkey, j)
            OP("vector", "tensor_scalar", reads=["l1a_ot", "l1a_stB"], writes=["l1a_of"],
               out=of_sb, in0=otc, scalar1=var3, scalar2=statsB[:, 2:3],
               op0=ALU.mult, op1=ALU.add)
            OP("vector", "tensor_add", reads=[xck, "l1a_of"], writes=[xck],
               out=xc, in0=xc, in1=of_sb)
    mk.barrier()
    _scB.close()

"""L1_ffn MoE block: x += LN0(moe(LN0(x))) — 8 experts, top-2 gates,
per-expert residual+LN+affine, gate-weighted sum. Dense (all experts).

Per 512-token tile:
  x_inner per t-chunk (fp32 t-major) -> fp32 transposes -> router (fp32
  matmuls, exact top-2) + xin_fm bf16.
  gates: m1/m2 via masked maxes; g1 = sigmoid(m1-m2), g2 = 1-g1;
  gates_tm [128, 8] -> transpose -> per-(chunk,expert) rows.
  per expert e: h = gelu(w1[e]^T xin + b1) (f-major, bf16);
    out2 = w2[e]^T h (two 4-bank halves); res = out2 + xin (bf16 store);
    row stats via ones-matmuls; rstd via transpose-dance reciprocal-free:
    rstd rows via Dsqrt-free Ln/Exp on columns... -> we use the
    transpose-dance with Ln/Exp on [128, 4] cols (small, batched per e).
    y += res*(gate*rstd)_bc * gamma_e  (+ rank-1 terms via K=8 matmul at
    tile end: sum_e [Q_e (x) gamma_e + gate_e (x) beta_e]).
  y -> transpose -> outer LN + residual.
"""


def emit_moe_prologue(ctx, din):
    nc, mk = ctx.nc, ctx.mk
    OP, DMA = ctx.OP, ctx.DMA
    s = {}
    ctx.moe = s
    s["w1"] = nc.dram_tensor("moe_w1b", [8, 1024, 4096], BF16, kind="Internal").ap()
    s["w2"] = nc.dram_tensor("moe_w2b", [8, 4096, 1024], BF16, kind="Internal").ap()
    s["b1c"] = ctx.sbuf("moe_b1c", [128, 8 * 32], F32)
    s["b2c"] = ctx.sbuf("moe_b2c", [128, 8 * 8], F32)
    s["gam"] = ctx.sbuf("moe_gam", [8, 1024], BF16)
    s["bet"] = ctx.sbuf("moe_bet", [8, 1024], BF16)
    s["gamc"] = ctx.sbuf("moe_gamc", [128, 8 * 8], F32)
    s["rw"] = ctx.sbuf("moe_rw", [128, 8 * 8], F32)
    DMA("sync", s["b1c"], din["moe_b1_col"], writes=["moe_b1c"])
    DMA("sync", s["b2c"], din["moe_b2_col"], writes=["moe_b2c"])
    DMA("sync", s["gam"], din["moe_gamma"], writes=["moe_gam"])
    DMA("sync", s["bet"], din["moe_beta"], writes=["moe_bet"])
    DMA("sync", s["gamc"], din["moe_gamma_col"], writes=["moe_gamc"])
    DMA("sync", s["rw"], din["moe_router_col"], writes=["moe_rw"])

    _scope = ExitStack()
    def scoped(name, shape, dt):
        return _scope.enter_context(nc.sbuf_tensor(name, shape, dt)).ap()

    win = scoped("moe_qin", [128, 4096], F32)
    wbf = scoped("moe_qbf", [128, 4096], BF16)
    for e in range(8):
        for c in range(8):
            DMA("sync", win, din["moe_w1"][e][c * 128:(c + 1) * 128, :],
                reads=["moe_qin"], writes=["moe_qin"], channel=1)
            OP("vector", "tensor_copy", reads=["moe_qin"], writes=["moe_qbf"],
               out=wbf, in_=win)
            DMA("sync", s["w1"][e][c * 128:(c + 1) * 128, :], wbf,
                reads=["moe_qbf"], writes=["moe_w1blob"], channel=2)
        for c in range(32):
            DMA("sync", win[:, 0:1024], din["moe_w2"][e][c * 128:(c + 1) * 128, :],
                reads=["moe_qin"], writes=["moe_qin"], channel=1)
            OP("vector", "tensor_copy", reads=["moe_qin"], writes=["moe_qbf"],
               out=wbf[:, 0:1024], in_=win[:, 0:1024])
            DMA("sync", s["w2"][e][c * 128:(c + 1) * 128, :], wbf[:, 0:1024],
                reads=["moe_qbf"], writes=["moe_w2blob"], channel=2)
    mk.barrier()
    _scope.close()


def emit_moe(ctx, xsrc, xdst):
    nc, mk = ctx.nc, ctx.mk
    OP, DMA = ctx.OP, ctx.DMA
    s = ctx.moe
    T = ctx.T
    NP = T // 1024
    _sc = ExitStack()
    def sc(name, shape, dt):
        return _sc.enter_context(nc.sbuf_tensor(name, shape, dt)).ap()

    xsr = xsrc.rearrange("(j p) d -> p j d", p=128)
    xdr = xdst.rearrange("(j p) d -> p j d", p=128)

    xt = [sc(f"moe_xt{i}", [128, 1024], F32) for i in range(2)]
    xin_bf = sc("moe_xinbf", [128, 8 * 1024], BF16)
    h_bf = sc("moe_h", [128, 32 * 1024], BF16)
    res_bf = sc("moe_res", [128, 8 * 1024], BF16)
    y_fm = sc("moe_y", [128, 8 * 1024], BF16)
    w1m = [sc(f"moe_w1m{i}", [128, 8 * 512], BF16) for i in range(2)]
    w2m = sc("moe_w2m", [128, 32 * 256], BF16)
    sscr = sc("moe_sscr", [128, 2048], F32)
    sqb = sc("moe_sqb", [128, 512], BF16)
    stats = sc("moe_st", [128, 8], F32)
    gates_tm = sc("moe_gtm", [128, 8 * 8], F32)
    glog = sc("moe_glog", [128, 8], F32)
    racc = sc("moe_racc", [1, 2048], F32)
    rows2 = sc("moe_rows2", [1, 1024], F32)
    cols32 = sc("moe_cols", [128, 32], F32)
    xin_f32 = sc("moe_xinf32", [128, 512], F32)
    qs_bf = sc("moe_qs", [8, 1024], BF16)
    gs_bf = sc("moe_gs", [8, 1024], BF16)
    rbf = sc("moe_rbf", [1, 2048], BF16)

    for it in range(NP):
        OP("vector", "memset", writes=["moe_y"], ap=y_fm, constant=0.0)
        for u in range(8):
            j = 8 * it + u
            xb, xk = xt[u % 2], ("moe_xt", u % 2)
            DMA("sync", xb, xsr[:, j], reads=[xk, ("xr", j)], writes=[xk], channel=3)
            mu, var = stats[:, 0:1], stats[:, 1:2]
            emit_ln_stats_tm(ctx, xb, xk, mu, var, "moe_st",
                             sscr[:, 1024:2048], "moe_sscr2", 1024)
            emit_rsqrt(ctx, var, var, "moe_st", LN_EPS)
            OP("vector", "scalar_tensor_tensor", reads=["moe_st"], writes=["moe_st"],
               out=stats[:, 2:3], in0=mu, scalar=-1.0, in1=var, op0=ALU.mult, op1=ALU.mult)
            OP("vector", "tensor_scalar", reads=[xk, "moe_st"], writes=["moe_sscr"],
               out=sscr[:, 0:1024], in0=xb, scalar1=var, scalar2=stats[:, 2:3],
               op0=ALU.mult, op1=ALU.add)
            for c in range(8):
                sl2 = xin_f32[:, (c % 2) * 128:(c % 2 + 1) * 128]
                transpose128(ctx, sl2, "moe_xinf32",
                             sscr[:, c * 128:(c + 1) * 128], "moe_sscr",
                             F32, pt_slot=c % 4)
                OP("tensor", "matmul", reads=["moe_xinf32", "moe_rw"], writes=["ps4"],
                   out=ctx.pb[4][:, 0:8], lhsT=sl2,
                   rhs=s["rw"][:, c * 8:(c + 1) * 8], start=(c == 0), stop=(c == 7))
                OP("vector", "tensor_copy", reads=["moe_xinf32"], writes=["moe_xin"],
                   out=xin_bf[:, c * 1024 + u * 128: c * 1024 + (u + 1) * 128],
                   in_=sl2)
            OP("vector", "tensor_copy", reads=["ps4"], writes=["moe_glog"],
               out=glog, in_=ctx.pb[4][:, 0:8])
            m1, m2, g1 = stats[:, 3:4], stats[:, 4:5], stats[:, 5:6]
            OP("vector", "reduce_max", reads=["moe_glog"], writes=["moe_st"],
               out=m1, in_=glog, axis=AX.X)
            gt_view = gates_tm.rearrange("p (e u) -> p u e", u=8)[:, u]
            OP("vector", "tensor_scalar", reads=["moe_glog", "moe_st"], writes=["moe_gtm"],
               out=gt_view, in0=glog, scalar1=m1, scalar2=None,
               op0=ALU.is_ge)
            OP("vector", "scalar_tensor_tensor", reads=["moe_gtm", "moe_glog"],
               writes=["moe_glog"],
               out=glog, in0=gt_view, scalar=-1e30, in1=glog,
               op0=ALU.mult, op1=ALU.add)
            OP("vector", "reduce_max", reads=["moe_glog"], writes=["moe_st"],
               out=m2, in_=glog, axis=AX.X)
            OP("vector", "tensor_sub", reads=["moe_st"], writes=["moe_st"],
               out=g1, in0=m1, in1=m2)
            OP("scalar", "activation", reads=["moe_st"], writes=["moe_st"],
               out=g1, in_=g1, func=AF.Sigmoid)
            OP("vector", "tensor_scalar", reads=["moe_glog", "moe_st"], writes=["moe_glog"],
               out=glog, in0=glog, scalar1=m2, scalar2=None, op0=ALU.is_ge)
            OP("vector", "tensor_scalar", reads=["moe_st"], writes=["moe_st"],
               out=stats[:, 6:7], in0=g1, scalar1=-1.0, scalar2=1.0,
               op0=ALU.mult, op1=ALU.add)
            OP("vector", "tensor_scalar", reads=["moe_glog", "moe_st"], writes=["moe_glog"],
               out=glog, in0=glog, scalar1=stats[:, 6:7], scalar2=None, op0=ALU.mult)
            OP("vector", "scalar_tensor_tensor",
               reads=["moe_gtm", "moe_st", "moe_glog"], writes=["moe_gtm"],
               out=gt_view, in0=gt_view,
               scalar=g1, in1=glog, op0=ALU.mult, op1=ALU.add)

        for e in range(8):
            OP("vector", "memset", writes=["moe_racc"], ap=racc, constant=0.0)
            for fb in range(8):
                wb, wk = w1m[fb % 2], ("moe_w1m", fb % 2)
                for c in range(8):
                    DMA("sync", wb[:, c * 512:(c + 1) * 512],
                        s["w1"][e][c * 128:(c + 1) * 128, fb * 512:(fb + 1) * 512],
                        reads=[wk, "moe_w1blob"], writes=[wk], channel=1 + fb % 2)
                for tb in range(2):
                    for c in range(8):
                        for fs in range(4):
                            OP("tensor", "matmul", reads=[wk, "moe_xin"], writes=[f"ps{fs}"],
                               out=ctx.pb[fs][:, 0:512],
                               lhsT=wb[:, c * 512 + fs * 128: c * 512 + (fs + 1) * 128],
                               rhs=xin_bf[:, c * 1024 + tb * 512: c * 1024 + (tb + 1) * 512],
                               start=(c == 0), stop=(c == 7))
                    for fs in range(4):
                        fc = fb * 4 + fs
                        OP("scalar", "activation", reads=[f"ps{fs}", "moe_b1c"],
                           writes=[("moe_h", fb)],
                           out=h_bf[:, fc * 1024 + tb * 512: fc * 1024 + (tb + 1) * 512],
                           in_=ctx.pb[fs][:, 0:512],
                           func=AF.Gelu, bias=s["b1c"][:, e * 32 + fc: e * 32 + fc + 1])
            for q in range(4):
                for hb in range(32):
                    DMA("sync", w2m[:, hb * 256:(hb + 1) * 256],
                        s["w2"][e][hb * 128:(hb + 1) * 128, q * 256:(q + 1) * 256],
                        reads=[("moe_w2m", hb), "moe_w2blob"],
                        writes=[("moe_w2m", hb)], channel=1 + hb % 2)
                    for cs in range(2):
                        for tb in range(2):
                            OP("tensor", "matmul",
                               reads=[("moe_w2m", hb), ("moe_h", hb // 4)],
                               writes=[f"ps{cs * 2 + tb}"],
                               out=ctx.pb[cs * 2 + tb][:, 0:512],
                               lhsT=w2m[:, hb * 256 + cs * 128: hb * 256 + (cs + 1) * 128],
                               rhs=h_bf[:, hb * 1024 + tb * 512: hb * 1024 + (tb + 1) * 512],
                               start=(hb == 0), stop=(hb == 31))
                for tb in range(2):
                    for cs in range(2):
                        c = q * 2 + cs
                        bank = cs * 2 + tb
                        rsl = res_bf[:, c * 1024 + tb * 512: c * 1024 + (tb + 1) * 512]
                        OP("vector", "tensor_scalar", reads=[f"ps{bank}", "moe_b2c"],
                           writes=["moe_sscr"],
                           out=sscr[:, 0:512], in0=ctx.pb[bank][:, 0:512],
                           scalar1=s["b2c"][:, e * 8 + c: e * 8 + c + 1], scalar2=None,
                           op0=ALU.add)
                        OP("vector", "tensor_tensor", reads=["moe_sscr", "moe_xin"],
                           writes=[("moe_res", c)],
                           out=rsl, in0=sscr[:, 0:512],
                           in1=xin_bf[:, c * 1024 + tb * 512: c * 1024 + (tb + 1) * 512],
                           op=ALU.add)
                        OP("scalar", "activation", reads=[("moe_res", c)], writes=["moe_sq"],
                           out=sqb, in_=rsl, func=AF.Square)
                        OP("tensor", "matmul", reads=[("moe_res", c), "c_onesb"],
                           writes=["ps4"],
                           out=ctx.pb[4][0:1, 0:512], lhsT=ctx.ones_col_bf,
                           rhs=rsl, start=(cs == 0), stop=(cs == 1))
                        OP("tensor", "matmul", reads=["moe_sq", "c_onesb"], writes=["ps5"],
                           out=ctx.pb[5][0:1, 0:512], lhsT=ctx.ones_col_bf,
                           rhs=sqb, start=(cs == 0), stop=(cs == 1))
                    if q == 0:
                        OP("vector", "tensor_copy", reads=["ps4"], writes=["moe_racc"],
                           out=racc[0:1, tb * 512:(tb + 1) * 512], in_=ctx.pb[4][0:1, 0:512])
                        OP("vector", "tensor_copy", reads=["ps5"], writes=["moe_racc"],
                           out=racc[0:1, 1024 + tb * 512: 1024 + (tb + 1) * 512],
                           in_=ctx.pb[5][0:1, 0:512])
                    else:
                        OP("vector", "tensor_add", reads=["ps4", "moe_racc"],
                           writes=["moe_racc"],
                           out=racc[0:1, tb * 512:(tb + 1) * 512],
                           in0=racc[0:1, tb * 512:(tb + 1) * 512],
                           in1=ctx.pb[4][0:1, 0:512])
                        OP("vector", "tensor_add", reads=["ps5", "moe_racc"],
                           writes=["moe_racc"],
                           out=racc[0:1, 1024 + tb * 512: 1024 + (tb + 1) * 512],
                           in0=racc[0:1, 1024 + tb * 512: 1024 + (tb + 1) * 512],
                           in1=ctx.pb[5][0:1, 0:512])
            mcol, vcol = cols32[:, 0:8], cols32[:, 8:16]
            tcol, pcol = cols32[:, 16:24], cols32[:, 24:32]
            for u8 in range(8):
                OP("tensor", "transpose", reads=["moe_racc", "c_idf"], writes=["ps4"],
                   out=ctx.pb[4][:, u8:u8 + 1],
                   in_=racc[0:1, u8 * 128:(u8 + 1) * 128],
                   identity=ctx.ident_f32[0:1, 0:1])
                OP("tensor", "transpose", reads=["moe_racc", "c_idf"], writes=["ps5"],
                   out=ctx.pb[5][:, u8:u8 + 1],
                   in_=racc[0:1, 1024 + u8 * 128: 1024 + (u8 + 1) * 128],
                   identity=ctx.ident_f32[0:1, 0:1])
            OP("vector", "tensor_scalar", reads=["ps4"], writes=["moe_cols"],
               out=mcol, in0=ctx.pb[4][:, 0:8], scalar1=1.0 / 1024, scalar2=None,
               op0=ALU.mult)
            OP("vector", "tensor_scalar", reads=["ps5"], writes=["moe_cols"],
               out=vcol, in0=ctx.pb[5][:, 0:8], scalar1=1.0 / 1024, scalar2=None,
               op0=ALU.mult)
            OP("vector", "scalar_tensor_tensor", reads=["moe_cols"], writes=["moe_cols"],
               out=tcol, in0=mcol, scalar=-1.0, in1=mcol, op0=ALU.mult, op1=ALU.mult)
            OP("vector", "tensor_add", reads=["moe_cols"], writes=["moe_cols"],
               out=vcol, in0=vcol, in1=tcol)
            emit_rsqrt(ctx, vcol, vcol, "moe_cols", 1e-6)
            OP("vector", "tensor_mul", reads=["moe_cols", "moe_gtm"], writes=["moe_cols"],
               out=pcol, in0=gates_tm[:, e * 8:(e + 1) * 8], in1=vcol)
            OP("vector", "scalar_tensor_tensor", reads=["moe_cols"], writes=["moe_cols"],
               out=tcol, in0=mcol, scalar=-1.0, in1=pcol, op0=ALU.mult, op1=ALU.mult)
            for half in range(2):
                for u4 in range(4):
                    u8 = half * 4 + u4
                    OP("tensor", "transpose", reads=["moe_cols", "c_idf"], writes=["ps4"],
                       out=ctx.pb[4][0:1, u4 * 128:(u4 + 1) * 128],
                       in_=pcol[:, u8:u8 + 1], identity=ctx.ident_f32)
                    OP("tensor", "transpose", reads=["moe_cols", "c_idf"], writes=["ps5"],
                       out=ctx.pb[5][0:1, u4 * 128:(u4 + 1) * 128],
                       in_=tcol[:, u8:u8 + 1], identity=ctx.ident_f32)
                OP("vector", "tensor_copy", reads=["ps4"], writes=["moe_rows2"],
                   out=rows2[0:1, half * 512:(half + 1) * 512], in_=ctx.pb[4][0:1, 0:512])
                OP("vector", "tensor_copy", reads=["ps5"], writes=["moe_rbf"],
                   out=rbf[0:1, half * 512:(half + 1) * 512], in_=ctx.pb[5][0:1, 0:512])
            for half in range(2):
                for u4 in range(4):
                    u8 = half * 4 + u4
                    OP("tensor", "transpose", reads=["moe_gtm", "c_idf"], writes=["ps5"],
                       out=ctx.pb[5][0:1, u4 * 128:(u4 + 1) * 128],
                       in_=gates_tm[:, e * 8 + u8: e * 8 + u8 + 1],
                       identity=ctx.ident_f32)
                OP("vector", "tensor_copy", reads=["ps5"], writes=["moe_rbf"],
                   out=rbf[0:1, 1024 + half * 512: 1024 + (half + 1) * 512],
                   in_=ctx.pb[5][0:1, 0:512])
            DMA("sync", qs_bf[e:e + 1, :], rbf[0:1, 0:1024], reads=["moe_rbf"],
                writes=["moe_qs"], channel=3)
            DMA("sync", gs_bf[e:e + 1, :], rbf[0:1, 1024:2048], reads=["moe_rbf"],
                writes=["moe_gs"], channel=3)
            for tb in range(2):
                OP("tensor", "matmul", reads=["moe_rows2", "c_1rowf"], writes=["ps4"],
                   out=ctx.pb[4][:, 0:512], lhsT=ctx.ones_row_f32,
                   rhs=rows2[0:1, tb * 512:(tb + 1) * 512], start=True, stop=True)
                OP("vector", "tensor_copy", reads=["ps4"], writes=["moe_sscr"],
                   out=sscr[:, tb * 512:(tb + 1) * 512], in_=ctx.pb[4][:, 0:512])
            for c in range(8):
                OP("vector", "tensor_tensor", reads=[("moe_res", c), "moe_sscr"],
                   writes=["moe_sscr2"],
                   out=sscr[:, 1024:2048], in0=res_bf[:, c * 1024:(c + 1) * 1024],
                   in1=sscr[:, 0:1024], op=ALU.mult)
                OP("vector", "scalar_tensor_tensor",
                   reads=["moe_sscr2", "moe_gamc", "moe_y"], writes=["moe_y"],
                   out=y_fm[:, c * 1024:(c + 1) * 1024], in0=sscr[:, 1024:2048],
                   scalar=s["gamc"][:, e * 8 + c: e * 8 + c + 1],
                   in1=y_fm[:, c * 1024:(c + 1) * 1024], op0=ALU.mult, op1=ALU.add)

        for c in range(8):
            for tb in range(2):
                OP("tensor", "matmul", reads=["moe_gam", "moe_qs"], writes=["ps4"],
                   out=ctx.pb[4][:, 0:512], lhsT=s["gam"][:, c * 128:(c + 1) * 128],
                   rhs=qs_bf[:, tb * 512:(tb + 1) * 512], start=True, stop=False)
                OP("tensor", "matmul", reads=["moe_bet", "moe_gs"], writes=["ps4"],
                   out=ctx.pb[4][:, 0:512], lhsT=s["bet"][:, c * 128:(c + 1) * 128],
                   rhs=gs_bf[:, tb * 512:(tb + 1) * 512], start=False, stop=True)
                ysl = y_fm[:, c * 1024 + tb * 512: c * 1024 + (tb + 1) * 512]
                OP("vector", "tensor_add", reads=["ps4", "moe_y"], writes=["moe_y"],
                   out=ysl, in0=ysl, in1=ctx.pb[4][:, 0:512])
        yt = h_bf[:, 0:8192]
        for c in range(8):
            for u in range(8):
                transpose128(ctx, yt[:, u * 1024 + c * 128: u * 1024 + (c + 1) * 128],
                             ("moe_h", u // 4),
                             y_fm[:, c * 1024 + u * 128: c * 1024 + (u + 1) * 128],
                             "moe_y", BF16, pt_idx=0, pt_slot=(c * 8 + u) % 8)
        for u in range(8):
            j = 8 * it + u
            otc = yt[:, u * 1024:(u + 1) * 1024]
            hk = ("moe_h", u // 4)
            mu3, var3 = stats[:, 0:1], stats[:, 1:2]
            emit_ln_stats_tm(ctx, otc, hk, mu3, var3, "moe_st",
                             sscr[:, 1024:2048], "moe_sscr2", 1024)
            emit_rsqrt(ctx, var3, var3, "moe_st", LN_EPS)
            OP("vector", "scalar_tensor_tensor", reads=["moe_st"], writes=["moe_st"],
               out=stats[:, 2:3], in0=mu3, scalar=-1.0, in1=var3, op0=ALU.mult, op1=ALU.mult)
            xb, xk = xt[u % 2], ("moe_xt", u % 2)
            DMA("sync", xb, xsr[:, j], reads=[xk, ("xr", j)], writes=[xk], channel=3)
            OP("vector", "tensor_scalar", reads=[hk, "moe_st"], writes=["moe_sscr"],
               out=sscr[:, 0:1024], in0=otc, scalar1=var3, scalar2=stats[:, 2:3],
               op0=ALU.mult, op1=ALU.add)
            OP("vector", "tensor_add", reads=[xk, "moe_sscr"], writes=[xk],
               out=xb, in0=xb, in1=sscr[:, 0:1024])
            DMA("sync", xdr[:, j], xb, reads=[xk], writes=[("xo", j)], channel=3)
    mk.barrier()
    _sc.close()

"""kernel() assembly: unfused (4 launches) or fused (1 launch) via FUSED flag."""

FUSED = True
T_FULL = 4096


def _dt(s):
    return {"float32": mybir.dt.float32, "bfloat16": mybir.dt.bfloat16}[s]


def _decl(nc, names, T):
    dr = {}
    for grp in names:
        for nm, shfn, dts in SPECS[grp]:
            dr[nm] = nc.dram_tensor(nm, shfn(T), _dt(dts), kind="ExternalInput").ap()
    return dr


def _stream_io(ctx, dr_x, dr_xout, x_sb):
    mk = ctx.mk
    xr = dr_x.rearrange("(j p) d -> p j d", p=128)
    for j in range(ctx.NT):
        mk.dma("sync", ("dma_start", dict(out=x_sb[:, j * 1024:(j + 1) * 1024],
                                          in_=xr[:, j])),
               writes=[("x", j)], channel=3)
    def store():
        xo = dr_xout.rearrange("(j p) d -> p j d", p=128)
        for j in range(ctx.NT):
            mk.dma("sync", ("dma_start", dict(out=xo[:, j],
                                              in_=x_sb[:, j * 1024:(j + 1) * 1024])),
                   reads=[("x", j)], channel=3)
    return store


def _build_block(block, T):
    nc = bass.Bass("TRN2", target_bir_lowering=False)
    dr = _decl(nc, ["common", block], T)
    x_in = nc.dram_tensor("x", [T, 1024], F32, kind="ExternalInput").ap()
    x_out = nc.dram_tensor("xout", [T, 1024], F32, kind="ExternalOutput").ap()
    mk = MK(nc)
    ctx = Ctx(nc, mk, T)
    setup_psum(ctx)
    setup_consts(ctx, dr["ident"], dr["ident_bf"])
    if block == "moe":
        scope = ExitStack()
        ctx.block_scope = scope
        emit_moe_prologue(ctx, dr)
        emit_moe(ctx, x_in, x_out)
        mk.barrier()
        ctx.block_scope = None
        scope.close()
        mk.finalize()
        return nc
    x_sb = ctx.sbuf("x_sb", [128, ctx.NT * 1024], F32)
    store = _stream_io(ctx, x_in, x_out, x_sb)
    mods = {"l0a": (emit_l0a_prologue, emit_l0a),
            "l0f": (emit_l0f_prologue, emit_l0f),
            "l1a": (emit_l1a_prologue, emit_l1a)}
    pro, emit = mods[block]
    scope = ExitStack()
    ctx.block_scope = scope
    pro(ctx, dr)
    emit(ctx, x_sb, "x")
    mk.barrier()
    ctx.block_scope = None
    scope.close()
    store()
    mk.finalize()
    return nc


def _build_fused(T):
    nc = bass.Bass("TRN2", target_bir_lowering=False)
    dr = _decl(nc, ["common", "l0a", "l0f", "l1a", "moe"], T)
    x_in = nc.dram_tensor("x", [T, 1024], F32, kind="ExternalInput").ap()
    x_out = nc.dram_tensor("xout", [T, 1024], F32, kind="ExternalOutput").ap()
    xres = nc.dram_tensor("xres", [T, 1024], F32, kind="Internal").ap()
    mk = MK(nc)
    ctx = Ctx(nc, mk, T)
    setup_psum(ctx)
    setup_consts(ctx, dr["ident"], dr["ident_bf"])
    xscope = ExitStack()
    x_sb = xscope.enter_context(nc.sbuf_tensor("x_sb", [128, ctx.NT * 1024], F32)).ap()
    _stream_io(ctx, x_in, x_out, x_sb)
    for pro, emit in [(emit_l0a_prologue, emit_l0a),
                      (emit_l0f_prologue, emit_l0f),
                      (emit_l1a_prologue, emit_l1a)]:
        scope = ExitStack()
        ctx.block_scope = scope
        pro(ctx, dr)
        emit(ctx, x_sb, "x")
        mk.barrier()
        ctx.block_scope = None
        scope.close()
    xr = xres.rearrange("(j p) d -> p j d", p=128)
    for j in range(ctx.NT):
        mk.dma("sync", ("dma_start", dict(out=xr[:, j],
                                          in_=x_sb[:, j * 1024:(j + 1) * 1024])),
               reads=[("x", j)], writes=[("xr", j)], channel=3)
    mk.barrier()
    xscope.close()
    scope = ExitStack()
    ctx.block_scope = scope
    emit_moe_prologue(ctx, dr)
    emit_moe(ctx, xres, x_out)
    mk.barrier()
    ctx.block_scope = None
    scope.close()
    mk.finalize()
    return nc


def kernel(**inputs):
    T = T_FULL
    x = np.ascontiguousarray(np.asarray(inputs["L0_attn_x"], dtype=np.float32))
    B = x.shape[0]
    d = host_prep(inputs, T)
    cur = [np.ascontiguousarray(x[i]) for i in range(B)]
    if FUSED:
        nc = _build_fused(T)
        keys = [nm for g in ("common", "l0a", "l0f", "l1a", "moe") for nm, _, _ in SPECS[g]]
        in_maps = [dict(x=cur[i], **{k: d[k] for k in keys}) for i in range(B)]
        res = run_bass_kernel_spmd(nc, in_maps, core_ids=list(range(B)))
        cur = [res.results[i]["xout"] for i in range(B)]
    else:
        for block in ("l0a", "l0f", "l1a", "moe"):
            nc = _build_block(block, T)
            keys = [nm for g in ("common", block) for nm, _, _ in SPECS[g]]
            in_maps = [dict(x=cur[i], **{k: d[k] for k in keys}) for i in range(B)]
            res = run_bass_kernel_spmd(nc, in_maps, core_ids=list(range(B)))
            cur = [res.results[i]["xout"] for i in range(B)]
    return np.stack(cur).astype(np.float32)
